# version 04
# baseline (speedup 1.0000x reference)
import numpy as np
from contextlib import ExitStack
import concourse.bass as bass
import concourse.mybir as mybir
from concourse.bass_utils import run_bass_kernel_spmd
from concourse.bass_core_rust import BassInstruction

F32 = mybir.dt.float32
BF16 = mybir.dt.bfloat16
AF = mybir.ActivationFunctionType
ALU = mybir.AluOpType

D = 1024
T = 4096
NCH = D // 128
TCH = 512
NTCH = T // TCH
EPS = 1e-5

N_STAGES = 8

DMA_SEM_NAMES = tuple(f"{k}{i}" for k in ("ld", "st", "wld", "gld") for i in range(2))


def fm(ap):
    return ap.rearrange("(c p) t -> p c t", p=128)


def vrow(ap):
    return ap.rearrange("(c p) -> p c", p=128)


_DRAM_CACHE = {}


def _dram(nc, name, shape, dt):
    key = (id(nc), name)
    if key not in _DRAM_CACHE:
        _DRAM_CACHE[key] = nc.dram_tensor(name, shape, dt, kind="Internal").ap()
    return _DRAM_CACHE[key]


_PASS_THROUGH = {"wait_ge", "drain", "nop", "dma_start", "indirect_dma_start",
                 "sem_inc", "engine_nop", "dma_start_transpose", "sem_clear"}


class Chained:

    def __init__(self, ctx, eng, semname):
        self._ctx, self._eng, self._semname = ctx, eng, semname

    def __getattr__(self, name):
        f = getattr(self._eng, name)
        if name.startswith("_") or name in _PASS_THROUGH or not callable(f):
            return f
        ctx, semname = self._ctx, self._semname

        def wrapped(*args, **kw):
            inst = f(*args, **kw)
            if isinstance(inst, BassInstruction):
                prev = ctx.cnt.get(semname, 0)
                sem = ctx.sem(semname)
                if prev > 0:
                    inst._wait_ge(sem, prev)
                inst.then_inc(sem, 1)
                ctx.inc(semname)
            return inst
        return wrapped


class Ctx:
    ENGS = ["sync", "gpsimd", "vector", "scalar", "tensor"]

    def __init__(self, nc, block, es):
        _CTX[0] = self
        self.nc, self.block, self.es = nc, block, es
        self.sems, self.cnt = {}, {}
        self.psum = es.enter_context(nc.psum_tensor("psum_pool", [128, 8, 512], F32))
        self.consts = es.enter_context(nc.sbuf_tensor("consts", [128, 16], F32))
        self.cconst, self.ncconst = {}, 0
        self.maskUT = es.enter_context(nc.sbuf_tensor("maskUT", [128, 128], BF16))
        self.ident_f = es.enter_context(nc.sbuf_tensor("ident_f", [128, 128], F32))
        self.ident_bf = es.enter_context(nc.sbuf_tensor("ident_bf", [128, 128], BF16))
        self.selmats = es.enter_context(nc.sbuf_tensor("selmats", [128, 8, 8], BF16))
        self.selrows = es.enter_context(nc.sbuf_tensor("selrows", [8, 8, 128], F32))
        self.ones1_f = es.enter_context(nc.sbuf_tensor("ones1_f", [1, 128], F32))
        self.ones2d = es.enter_context(nc.sbuf_tensor("ones2d", [128, 64], F32))

        def emitg(g):
            g.memset(self.ident_f[:], 0.0)
            g.affine_select(out=self.ident_f[:], in_=self.ident_f[:],
                            compare_op=ALU.not_equal, fill=1.0, base=0,
                            pattern=[[-1, 128]], channel_multiplier=1)
            g.memset(self.maskUT[:], 0.0)
            g.affine_select(out=self.maskUT[:], in_=self.maskUT[:],
                            compare_op=ALU.is_gt, fill=1.0, base=0,
                            pattern=[[-1, 128]], channel_multiplier=1)
            g.memset(self.ones1_f[:], 1.0)
            g.memset(self.ones2d[:], 1.0)
            g.memset(self.selmats[:], 0.0)
            for j in range(8):
                g.memset(self.selmats[:, j, j:j + 1], 1.0)
            g.memset(self.selrows[:], 0.0)
            g.affine_select(out=self.selrows[:], in_=self.selrows[:],
                            compare_op=ALU.not_equal, fill=1.0, base=0,
                            pattern=[[-1, 8], [0, 128]], channel_multiplier=1)
        self.gps(emitg)
        gv = self.cnt["gself"]

        def emitv(v):
            v.wait_ge(self.sem("gself"), gv)
            v.tensor_copy(self.ident_bf[:], self.ident_f[:])
        self.vec(emitv)
        self.barrier()

    _uid = 0

    def vec(self, fn):
        self.block.vector(lambda v: fn(Chained(self, v, "vself")))

    def scl(self, fn):
        self.block.scalar(lambda s: fn(Chained(self, s, "aself")))

    def gps(self, fn):
        self.block.gpsimd(lambda g: fn(Chained(self, g, "gself")))

    def tns(self, fn):
        self.block.tensor(fn)

    def syn(self, fn):
        self.block.sync(fn)

    def sbuf(self, es2, name, shape, dt=F32):
        Ctx._uid += 1
        return es2.enter_context(self.nc.sbuf_tensor(f"{name}_{Ctx._uid}", shape, dt))

    def sem(self, name):
        if name not in self.sems:
            self.sems[name] = self.es.enter_context(self.nc.semaphore(name))
            self.cnt[name] = 0
        return self.sems[name]

    def inc(self, name, n=1):
        self.sem(name)
        self.cnt[name] += n
        return self.cnt[name]

    def selcol(self, tc):
        return self.selmats[:, tc, :]

    def barrier(self, engs=None):
        engs = engs or self.ENGS
        dma_waits = [(s, self.cnt[s]) for s in DMA_SEM_NAMES
                     if s in self.sems and self.cnt[s] > 0]
        targets = {e: self.inc("bar_" + e) for e in engs}
        for e in engs:
            def emit(eng, e=e):
                if e == "sync":
                    for s, v in dma_waits:
                        eng.wait_ge(self.sems[s], v)
                eng.drain().then_inc(self.sem("bar_" + e), 1)
                for o in engs:
                    if o != e:
                        eng.wait_ge(self.sem("bar_" + o), targets[o])
            getattr(self.block, e)(emit)

    def reset_sems(self):
        self.barrier()
        names = [n for n in self.sems if not n.startswith("bar_")]

        def emitg(g):
            for n in names:
                num = self.sems[n].num
                g.dma_reset(range(num, num + 1))
                g.sem_clear(self.sems[n])
        self.block.gpsimd(emitg)
        for n in names:
            self.cnt[n] = 0
        for k in list(self.cnt):
            if k.startswith("_seq_"):
                self.cnt[k] = 0
        self.barrier()

    def constcol(self, value):
        value = float(value)
        if value not in self.cconst:
            idx = self.ncconst
            assert idx < 16
            self.ncconst += 1
            self.cconst[value] = idx

            def emit(g):
                g.memset(self.consts[:, idx:idx + 1], value)
            self.gps(emit)
        return self.consts[:, self.cconst[value]:self.cconst[value] + 1]



def wge(e, marks):
    if not marks:
        return
    if isinstance(marks[0], str):
        marks = [marks]
    for s, v in marks:
        if v > 0:
            e.wait_ge(_CTX[0].sem(s), v)


_CTX = [None]


def dma_issue(ctx, kind, eng, out, in_, slow=False, wait=None, transpose=False):
    seq = ctx.cnt.get("_seq_" + kind, 0)
    ctx.cnt["_seq_" + kind] = seq + 1
    sem = f"{kind}{seq % 2}"
    tgt = ctx.inc(sem, 16)

    def emit(e):
        if wait is not None:
            wge(e, wait)
        if tgt - 16 > 0:
            e.wait_ge(ctx.sem(sem), tgt - 16)
        e.dma_start(out=out, in_=in_, transpose=transpose,
                    allow_slow_non_contiguous=slow).then_inc(ctx.sem(sem), 16)
    getattr(ctx.block, eng)(emit)
    return (sem, tgt)


def dma_mark(ctx, kind):
    return [(f"{kind}{i}", ctx.cnt.get(f"{kind}{i}", 0)) for i in range(2)
            if ctx.cnt.get(f"{kind}{i}", 0) > 0]


def load_weight(ctx, w_hbm, sb, n_in, n_out):
    ncin = n_in // 128
    wv = w_hbm.rearrange("(c p) o -> p c o", p=128)
    for c in range(ncin):
        dma_issue(ctx, "wld", "gpsimd", sb[:, c, :], wv[:, c, :])
    return dma_mark(ctx, "wld")


def load_tile(ctx, sb_ap, hbm_ap, slow=False, eng="sync", wait=None):
    kind = "gld" if eng == "gpsimd" else "ld"
    dma_issue(ctx, kind, eng, sb_ap, hbm_ap, slow=slow, wait=wait)
    return dma_mark(ctx, kind)


def store_tile(ctx, hbm_ap, sb_ap, wait_sem=None, wait_val=None):
    w = [(wait_sem, wait_val)] if wait_sem is not None else None
    dma_issue(ctx, "st", "sync", hbm_ap, sb_ap, wait=w)
    return dma_mark(ctx, "st")


def emit_ln(ctx, src, dst, gamma=None, beta=None, rms=False, src_bf16=False,
            n_feat=D, eps=EPS):
    nch = n_feat // 128
    es2 = ExitStack()
    dt_in = BF16 if src_bf16 else F32
    xs = ctx.sbuf(es2, "ln_x", [128, T], dt_in)
    xb = ctx.sbuf(es2, "ln_xb", [128, T], BF16)
    sq = ctx.sbuf(es2, "ln_sq", [128, T], BF16)
    stats = ctx.sbuf(es2, "ln_stats", [8, 6, TCH], F32)
    asb = ctx.sbuf(es2, "ln_a", [128, T], F32)
    csb = ctx.sbuf(es2, "ln_c", [128, T], F32)
    gb = ctx.sbuf(es2, "ln_gb", [128, 2, nch], F32)
    ysb = ctx.sbuf(es2, "ln_y", [128, T], BF16)
    srcv = src.rearrange("(c p) t -> p c t", p=128)
    dstv = dst.rearrange("(c p) t -> p c t", p=128)
    epscol = ctx.constcol(eps)

    if gamma is not None:
        load_tile(ctx, gb[:, 0, :], vrow(gamma), slow=True)
        if beta is not None:
            load_tile(ctx, gb[:, 1, :], vrow(beta), slow=True)
    ctx.barrier()

    psum_s = ctx.psum[0:8, 0, :]
    psum_q = ctx.psum[0:8, 1, :]

    for c in range(nch):
        ldv = load_tile(ctx, xs[:], srcv[:, c, :])

        def emitv(v, ldv=ldv):
            wge(v, ldv)
            v.tensor_copy(xb[:], xs[:])
        ctx.vec(emitv)
        vv = ctx.cnt["vself"]

        def emits(s, vv=vv):
            s.wait_ge(ctx.sem("vself"), vv)
            s.activation(sq[:], xb[:], AF.Square)
        ctx.scl(emits)
        av = ctx.cnt["aself"]

        def emitt(t, c=c, av=av):
            t.wait_ge(ctx.sem("aself"), av)
            for tc in range(NTCH):
                first = (c == 0 and tc == 0)
                last = (c == nch - 1 and tc == NTCH - 1)
                t.matmul(psum_s, ctx.selcol(tc), xb[:, tc * TCH:(tc + 1) * TCH],
                         start=first, stop=last)
                t.matmul(psum_q, ctx.selcol(tc), sq[:, tc * TCH:(tc + 1) * TCH],
                         start=first, stop=last)
            t.drain().then_inc(ctx.sem("t"), 1)
        ctx.tns(emitt)
        ctx.inc("t")
        ctx.barrier(["sync", "vector", "scalar", "tensor"])

    inv_n = 1.0 / n_feat

    def emitstat(v):
        v.tensor_scalar(stats[:, 2, :], psum_s, inv_n, None, ALU.mult)
        v.tensor_scalar(stats[:, 5, :], psum_q, inv_n, None, ALU.mult)
        if rms:
            v.tensor_copy(stats[:, 0, :], stats[:, 5, :])
        else:
            v.tensor_tensor(stats[:, 1, :], stats[:, 2, :], stats[:, 2, :], ALU.mult)
            v.tensor_tensor(stats[:, 0, :], stats[:, 5, :], stats[:, 1, :], ALU.subtract)
    ctx.vec(emitstat)
    vv = ctx.cnt["vself"]

    def emitstat2(s, vv=vv):
        s.wait_ge(ctx.sem("vself"), vv)
        s.activation(stats[:, 4, :], stats[:, 0, :], AF.Ln, bias=epscol[0:8, :])
        s.activation(stats[:, 3, :], stats[:, 4, :], AF.Exp, scale=-0.5)
    ctx.scl(emitstat2)
    av = ctx.cnt["aself"]

    def emitstat3(v, av=av):
        v.wait_ge(ctx.sem("aself"), av)
        if rms:
            v.memset(stats[:, 4, :], 0.0)
        else:
            v.tensor_tensor(stats[:, 4, :], stats[:, 2, :], stats[:, 3, :], ALU.mult)
    ctx.vec(emitstat3)
    vv = ctx.cnt["vself"]

    for tc in range(NTCH):
        def emitbc(t, tc=tc, vv=vv):
            if tc == 0:
                t.wait_ge(ctx.sem("vself"), vv)
            t.matmul(ctx.psum[:, 2, :], ctx.selrows[:, tc, :], stats[0:8, 3, :], start=True, stop=True)
            t.matmul(ctx.psum[:, 3, :], ctx.selrows[:, tc, :], stats[0:8, 4, :], start=True, stop=True)
            t.drain().then_inc(ctx.sem("t"), 1)
        ctx.tns(emitbc)
        tv = ctx.inc("t")

        def emitbc2(v, tc=tc, tv=tv):
            v.wait_ge(ctx.sem("t"), tv)
            v.tensor_copy(asb[:, tc * TCH:(tc + 1) * TCH], ctx.psum[:, 2, :])
            v.tensor_copy(csb[:, tc * TCH:(tc + 1) * TCH], ctx.psum[:, 3, :])
        ctx.vec(emitbc2)
        ctx.barrier(["vector", "tensor"])

    for c in range(nch):
        ldv = load_tile(ctx, xs[:], srcv[:, c, :])

        def emitv(v, c=c, ldv=ldv):
            wge(v, ldv)
            v.tensor_tensor(xb[:], xs[:], asb[:], ALU.mult)
            v.tensor_tensor(ysb[:], xb[:], csb[:], ALU.subtract)
            if gamma is not None and beta is not None:
                v.tensor_scalar(ysb[:], ysb[:], gb[:, 0, c:c + 1], gb[:, 1, c:c + 1],
                                ALU.mult, ALU.add)
            elif gamma is not None:
                v.tensor_scalar(ysb[:], ysb[:], gb[:, 0, c:c + 1], None, ALU.mult)
        ctx.vec(emitv)
        store_tile(ctx, dstv[:, c, :], ysb[:], "vself", ctx.cnt["vself"])
        ctx.barrier(["sync", "vector"])
    ctx.barrier()
    es2.close()


def emit_proj_fm(ctx, w_hbm, n_in, n_out, src, dst, bias=None, act=None,
                 res=None, dst_f32=None, extra_add=None, scale=None, post_affine=None,
                 bias_const=None):
    es2 = ExitStack()
    ncin = n_in // 128
    nm = n_out // 128
    wsb = ctx.sbuf(es2, "pj_w", [128, ncin, n_out], BF16)
    load_weight(ctx, w_hbm, wsb, n_in, n_out)
    ysb = ctx.sbuf(es2, "pj_y", [128, ncin, TCH], BF16)
    osb = ctx.sbuf(es2, "pj_o", [128, TCH], BF16)
    bsb = ctx.sbuf(es2, "pj_b", [128, max(nm, 1)], F32)
    rsb = ctx.sbuf(es2, "pj_r", [128, TCH], F32)
    esb = ctx.sbuf(es2, "pj_e", [128, TCH], BF16)
    srcv = src.rearrange("(c p) t -> p c t", p=128)
    dstv = dst.rearrange("(c p) t -> p c t", p=128) if dst is not None else None
    dfv = dst_f32.rearrange("(c p) t -> p c t", p=128) if dst_f32 is not None else None
    resv = res.rearrange("(c p) t -> p c t", p=128) if res is not None else None
    exv = extra_add.rearrange("(c p) t -> p c t", p=128) if extra_add is not None else None
    if bias is not None:
        load_tile(ctx, bsb[:, 0:nm], vrow(bias), slow=True)
    bcol = ctx.constcol(bias_const) if bias_const is not None else None
    ctx.barrier()
    func = act if act is not None else (AF.Identity if (bias is not None or scale is not None) else AF.Copy)

    csem = "vself" if (res is not None or dst_f32 is not None) else "aself"
    for tc in range(NTCH):
        ldv = 0
        for c in range(ncin):
            ldv = load_tile(ctx, ysb[:, c, :], srcv[:, c, tc * TCH:(tc + 1) * TCH])
        bank_free = {}
        for m in range(nm):
            bank = m % 4 + 4
            pewait = bank_free.get(m - 4, 0)

            def emitt(t, m=m, bank=bank, ldv=ldv, pewait=pewait):
                if m == 0:
                    wge(t, ldv)
                if pewait > 0:
                    t.wait_ge(ctx.sem(csem), pewait)
                for c in range(ncin):
                    t.matmul(ctx.psum[:, bank, :], wsb[:, c, m * 128:(m + 1) * 128],
                             ysb[:, c, :], start=(c == 0), stop=(c == ncin - 1))
                t.drain().then_inc(ctx.sem("t"), 1)
            ctx.tns(emitt)
            tv = ctx.inc("t")

            if res is None and dst_f32 is not None:
                stw = dma_mark(ctx, "st")

                def emitvz(v, bank=bank, tv=tv, stw=stw):
                    v.wait_ge(ctx.sem("t"), tv)
                    wge(v, stw)
                    v.tensor_copy(rsb[:], ctx.psum[:, bank, :])
                ctx.vec(emitvz)
                bank_free[m] = ctx.cnt["vself"]
                store_tile(ctx, dfv[:, m, tc * TCH:(tc + 1) * TCH], rsb[:],
                           "vself", ctx.cnt["vself"])
            elif res is None:
                stw = dma_mark(ctx, "st")

                def emita(s, m=m, bank=bank, tv=tv, stw=stw):
                    s.wait_ge(ctx.sem("t"), tv)
                    wge(s, stw)
                    kwargs = {}
                    if bias is not None:
                        kwargs["bias"] = bsb[:, m:m + 1]
                    elif bcol is not None:
                        kwargs["bias"] = bcol
                    if scale is not None:
                        kwargs["scale"] = scale
                    s.activation(osb[:], ctx.psum[:, bank, :], func, **kwargs)
                    if post_affine is not None:
                        s.activation(osb[:], osb[:], AF.Identity,
                                     scale=post_affine[0], bias=post_affine[1])
                ctx.scl(emita)
                bank_free[m] = ctx.cnt["aself"]
                store_tile(ctx, dstv[:, m, tc * TCH:(tc + 1) * TCH], osb[:],
                           "aself", ctx.cnt["aself"])
            else:
                res_is_f32 = (res.tensor.dtype == F32)
                if res_is_f32:
                    rldv = load_tile(ctx, rsb[:], resv[:, m, tc * TCH:(tc + 1) * TCH], wait=dma_mark(ctx, "st"))
                else:
                    rldv = load_tile(ctx, rsb[:], resv[:, m, tc * TCH:(tc + 1) * TCH], eng="gpsimd", wait=dma_mark(ctx, "st"))
                if exv is not None:
                    rldv2 = load_tile(ctx, esb[:], exv[:, m, tc * TCH:(tc + 1) * TCH], wait=dma_mark(ctx, "st"))
                else:
                    rldv2 = None
                stw = dma_mark(ctx, "st")

                def emitv(v, bank=bank, tv=tv, rldv=rldv, rldv2=rldv2, stw=stw):
                    v.wait_ge(ctx.sem("t"), tv)
                    wge(v, rldv)
                    if rldv2 is not None:
                        wge(v, rldv2)
                    wge(v, stw)
                    v.tensor_tensor(rsb[:], rsb[:], ctx.psum[:, bank, :], ALU.add)
                    if exv is not None:
                        v.tensor_tensor(rsb[:], rsb[:], esb[:], ALU.add)
                ctx.vec(emitv)
                bank_free[m] = ctx.cnt["vself"]
                store_tile(ctx, dfv[:, m, tc * TCH:(tc + 1) * TCH], rsb[:],
                           "vself", ctx.cnt["vself"])
        ctx.barrier(["sync", "gpsimd", "vector", "scalar", "tensor"])
    ctx.barrier()
    es2.close()


def emit_proj_tm(ctx, w_hbm, n_in, n_out, src, dst):
    es2 = ExitStack()
    ncin = n_in // 128
    nn = max(1, n_out // 512)
    nw = min(512, n_out)
    wsb = ctx.sbuf(es2, "pt_w", [128, ncin, n_out], BF16)
    load_weight(ctx, w_hbm, wsb, n_in, n_out)
    ysb = ctx.sbuf(es2, "pt_y", [128, ncin, 128], BF16)
    osb = ctx.sbuf(es2, "pt_o", [128, n_out], BF16)
    srcv = src.rearrange("(c p) t -> p c t", p=128)
    ctx.barrier()
    for tt in range(T // 128):
        ldv = 0
        for c in range(ncin):
            ldv = load_tile(ctx, ysb[:, c, :], srcv[:, c, tt * 128:(tt + 1) * 128])
        for n in range(nn):
            bank = n % 4 + 4

            def emitt(t, n=n, bank=bank, ldv=ldv):
                if n == 0:
                    wge(t, ldv)
                for c in range(ncin):
                    t.matmul(ctx.psum[:, bank, 0:nw], ysb[:, c, :],
                             wsb[:, c, n * nw:(n + 1) * nw], start=(c == 0), stop=(c == ncin - 1))
                t.drain().then_inc(ctx.sem("t"), 1)
            ctx.tns(emitt)
            tv = ctx.inc("t")

            def emita(s, n=n, bank=bank, tv=tv):
                s.wait_ge(ctx.sem("t"), tv)
                s.activation(osb[:, n * nw:(n + 1) * nw], ctx.psum[:, bank, 0:nw], AF.Copy)
            ctx.scl(emita)
        store_tile(ctx, dst[tt * 128:(tt + 1) * 128, :], osb[:], "aself", ctx.cnt["aself"])
        ctx.barrier(["sync", "scalar", "tensor"])
    ctx.barrier()
    es2.close()


def emit_linattn(ctx, QF, KF, KT, VT, OF):
    es2 = ExitStack()
    CH = 128
    NH = 16
    qc = ctx.sbuf(es2, "la_q", [128, NCH, CH], BF16)
    kc = ctx.sbuf(es2, "la_k", [128, NCH, CH], BF16)
    ktc = ctx.sbuf(es2, "la_kt", [128, NH, 64], BF16)
    vtc = ctx.sbuf(es2, "la_vt", [128, NH, 64], BF16)
    msb = ctx.sbuf(es2, "la_m", [128, NH // 2, 64], BF16)
    msf = ctx.sbuf(es2, "la_mf", [128, NH // 2, 64], F32)
    ssb = ctx.sbuf(es2, "la_s", [128, 2, CH], BF16)
    osb = ctx.sbuf(es2, "la_o", [128, NCH, CH], BF16)
    qv, kv, ofv = fm(QF), fm(KF), fm(OF)

    def emitz(g):
        g.memset(msf[:], 0.0)
        g.memset(msb[:], 0.0)
    ctx.gps(emitz)
    ctx.barrier()

    def o_psum(ph):
        return ctx.psum[:, 2 + ph // 4, (ph % 4) * 128:(ph % 4) * 128 + 128]

    def du_psum(ph):
        return ctx.psum[:, 6 + ph // 4, (ph % 4) * 64:(ph % 4) * 64 + 64]

    for c in range(T // CH):
        load_tile(ctx, qc[:], qv[:, :, c * CH:(c + 1) * CH])
        load_tile(ctx, kc[:], kv[:, :, c * CH:(c + 1) * CH])
        load_tile(ctx, ktc[:], KT[c * CH:(c + 1) * CH, :].rearrange("p (h d) -> p h d", d=64))
        ldv = load_tile(ctx, vtc[:], VT[c * CH:(c + 1) * CH, :].rearrange("p (h d) -> p h d", d=64))

        for h in range(NH):
            ph, hp, cd = h // 2, (h % 2) * 64, h // 2
            sbank = h % 2
            vwait = ctx.cnt["vself"] - 1 if h >= 2 else 0

            def emit_sc(t, h=h, hp=hp, cd=cd, sbank=sbank, ldv=ldv, vwait=vwait):
                if h == 0:
                    wge(t, ldv)
                if vwait > 0:
                    t.wait_ge(ctx.sem("vself"), vwait)
                t.matmul(ctx.psum[:, sbank, 0:CH], kc[hp:hp + 64, cd, :],
                         qc[hp:hp + 64, cd, :], start=True, stop=True,
                         tile_position=(hp, 0))
                t.drain().then_inc(ctx.sem("t"), 1)
            ctx.tns(emit_sc)
            tv = ctx.inc("t")
            t2wait = ctx.cnt["t2"] - 1 if h >= 2 else 0

            def emit_mask(v, h=h, sbank=sbank, tv=tv, t2wait=t2wait):
                v.wait_ge(ctx.sem("t"), tv)
                if t2wait > 0:
                    v.wait_ge(ctx.sem("t2"), t2wait)
                v.tensor_tensor(ssb[:, h % 2, :], ctx.psum[:, sbank, 0:CH],
                                ctx.maskUT[:], ALU.mult)
            ctx.vec(emit_mask)
            mv = ctx.cnt["vself"]

            def emit_o(t, h=h, ph=ph, hp=hp, cd=cd, mv=mv):
                t.wait_ge(ctx.sem("vself"), mv)
                t.matmul(o_psum(ph)[hp:hp + 64, :], msb[hp:hp + 64, ph, :],
                         qc[hp:hp + 64, cd, :], start=True, stop=False,
                         tile_position=(hp, hp))
                t.matmul(o_psum(ph)[hp:hp + 64, :], vtc[:, h, :], ssb[:, h % 2, :],
                         start=False, stop=True, tile_position=(0, hp))
                t.matmul(du_psum(ph)[hp:hp + 64, :], ktc[:, h, :], vtc[:, h, :],
                         start=True, stop=True, tile_position=(0, hp))
                t.drain().then_inc(ctx.sem("t2"), 1)
            ctx.tns(emit_o)
            ctx.inc("t2")

        t2v = ctx.cnt["t2"]

        def emit_fin(v, t2v=t2v):
            v.wait_ge(ctx.sem("t2"), t2v)
            v.tensor_tensor(msf[:, 0:4, :].rearrange("p a b -> p (a b)"),
                            msf[:, 0:4, :].rearrange("p a b -> p (a b)"),
                            ctx.psum[:, 6, 0:256], ALU.add)
            v.tensor_tensor(msf[:, 4:8, :].rearrange("p a b -> p (a b)"),
                            msf[:, 4:8, :].rearrange("p a b -> p (a b)"),
                            ctx.psum[:, 7, 0:256], ALU.add)
            v.tensor_copy(msb[:], msf[:])
            for ph in range(8):
                v.tensor_copy(osb[:, ph, :], o_psum(ph))
        ctx.vec(emit_fin)
        store_tile(ctx, ofv[:, :, c * CH:(c + 1) * CH], osb[:], "vself", ctx.cnt["vself"])
        ctx.barrier(["sync", "vector", "tensor"])
    ctx.barrier()
    es2.close()


def emit_add3(ctx, dst_f32, a_f32, b_bf=None, c_bf=None):
    es2 = ExitStack()
    xa = ctx.sbuf(es2, "ad_a", [128, T], F32)
    xb = ctx.sbuf(es2, "ad_b", [128, T], BF16)
    xc = ctx.sbuf(es2, "ad_c", [128, T], BF16)
    av_, dv_ = fm(a_f32), fm(dst_f32)
    bv_ = fm(b_bf) if b_bf is not None else None
    cv_ = fm(c_bf) if c_bf is not None else None
    for c in range(NCH):
        ldv = load_tile(ctx, xa[:], av_[:, c, :])
        if bv_ is not None:
            ldv = load_tile(ctx, xb[:], bv_[:, c, :])
        if cv_ is not None:
            ldv = load_tile(ctx, xc[:], cv_[:, c, :])

        def emitv(v, ldv=ldv):
            wge(v, ldv)
            if bv_ is not None:
                v.tensor_tensor(xa[:], xa[:], xb[:], ALU.add)
            if cv_ is not None:
                v.tensor_tensor(xa[:], xa[:], xc[:], ALU.add)
        ctx.vec(emitv)
        store_tile(ctx, dv_[:, c, :], xa[:], "vself", ctx.cnt["vself"])
        ctx.barrier(["sync", "vector"])
    ctx.barrier()
    es2.close()


def emit_cast(ctx, src_f32, dst_bf):
    es2 = ExitStack()
    xa = ctx.sbuf(es2, "cs_a", [128, T], F32)
    xb = ctx.sbuf(es2, "cs_b", [128, T], BF16)
    sv, dv = fm(src_f32), fm(dst_bf)
    for c in range(NCH):
        ldv = load_tile(ctx, xa[:], sv[:, c, :])

        def emitv(v, ldv=ldv):
            wge(v, ldv)
            v.tensor_copy(xb[:], xa[:])
        ctx.vec(emitv)
        store_tile(ctx, dv[:, c, :], xb[:], "vself", ctx.cnt["vself"])
        ctx.barrier(["sync", "vector"])
    ctx.barrier()
    es2.close()




def emit_gcnet(ctx, Y, w, gf, s0, s1, arsb):
    es2 = ExitStack()
    yt = ctx.sbuf(es2, "gc_y", [128, T], BF16)
    gv = ctx.sbuf(es2, "gc_gv", [128, 8], F32)
    gvb = ctx.sbuf(es2, "gc_gvb", [128, 8], BF16)
    h1 = ctx.sbuf(es2, "gc_h1", [128, 4], BF16)
    h2 = ctx.sbuf(es2, "gc_h2", [128, 2], BF16)
    w1 = ctx.sbuf(es2, "gc_w1", [128, 8, 512], BF16)
    w2 = ctx.sbuf(es2, "gc_w2", [128, 4, 256], BF16)
    w3 = ctx.sbuf(es2, "gc_w3", [128, 2, 1], BF16)
    b1 = ctx.sbuf(es2, "gc_b1", [128, 4], F32)
    b2 = ctx.sbuf(es2, "gc_b2", [128, 2], F32)
    b3 = ctx.sbuf(es2, "gc_b3", [1, 1], F32)
    tmp = ctx.sbuf(es2, "gc_tmp", [1, 4], F32)
    load_weight(ctx, w["gc_w1"], w1, D, 512)
    load_weight(ctx, w["gc_w2"], w2, 512, 256)
    load_weight(ctx, w["gc_w3"], w3, 256, 1)
    load_tile(ctx, b1[:, :], vrow(w["gc_b1"]), slow=True)
    load_tile(ctx, b2[:, :], vrow(w["gc_b2"]), slow=True)
    load_tile(ctx, b3[:, :], w["gc_b3"].rearrange("(a b) -> a b", a=1))
    yv = fm(Y)
    ctx.barrier()
    for c in range(NCH):
        ldv = load_tile(ctx, yt[:], yv[:, c, :])

        def emitv(v, c=c, ldv=ldv):
            wge(v, ldv)
            v.reduce_sum(gv[:, c:c + 1], yt[:], axis=mybir.AxisListType.X)
            v.tensor_scalar(gvb[:, c:c + 1], gv[:, c:c + 1], 1.0 / T, None, ALU.mult)
        ctx.vec(emitv)
        ctx.barrier(["sync", "vector"])
    vv = ctx.cnt["vself"]

    def emitt(t, vv=vv):
        t.wait_ge(ctx.sem("vself"), vv)
        for m in range(4):
            for c in range(NCH):
                t.matmul(ctx.psum[:, 0, m:m + 1], w1[:, c, m * 128:(m + 1) * 128],
                         gvb[:, c:c + 1], start=(c == 0), stop=(c == NCH - 1))
        t.drain().then_inc(ctx.sem("t"), 1)
    ctx.tns(emitt)
    tv = ctx.inc("t")

    def emita(s, tv=tv):
        s.wait_ge(ctx.sem("t"), tv)
        s.activation(h1[:, :], ctx.psum[:, 0, 0:4], AF.Gelu, bias=b1[:, 0:4].rearrange("p a -> p a")[:, 0:1]) if False else None
        for m in range(4):
            s.activation(h1[:, m:m + 1], ctx.psum[:, 0, m:m + 1], AF.Gelu, bias=b1[:, m:m + 1])
    ctx.scl(emita)
    av = ctx.cnt["aself"]

    def emitt2(t, av=av):
        t.wait_ge(ctx.sem("aself"), av)
        for m in range(2):
            for c in range(4):
                t.matmul(ctx.psum[:, 1, m:m + 1], w2[:, c, m * 128:(m + 1) * 128],
                         h1[:, c:c + 1], start=(c == 0), stop=(c == 3))
        t.drain().then_inc(ctx.sem("t"), 1)
    ctx.tns(emitt2)
    tv = ctx.inc("t")

    def emita2(s, tv=tv):
        s.wait_ge(ctx.sem("t"), tv)
        for m in range(2):
            s.activation(h2[:, m:m + 1], ctx.psum[:, 1, m:m + 1], AF.Gelu, bias=b2[:, m:m + 1])
    ctx.scl(emita2)
    av = ctx.cnt["aself"]

    def emitt3(t, av=av):
        t.wait_ge(ctx.sem("aself"), av)
        for c in range(2):
            t.matmul(ctx.psum[0:1, 2, 0:1], w3[:, c, 0:1], h2[:, c:c + 1],
                     start=(c == 0), stop=(c == 1))
        t.drain().then_inc(ctx.sem("t"), 1)
    ctx.tns(emitt3)
    tv = ctx.inc("t")

    def emita3(s, tv=tv):
        s.wait_ge(ctx.sem("t"), tv)
        s.activation(gf[0:1, :], ctx.psum[0:1, 2, 0:1], AF.Sigmoid, bias=b3[0:1, :])
    ctx.scl(emita3)
    av = ctx.cnt["aself"]

    def emitv2(v, av=av):
        v.wait_ge(ctx.sem("aself"), av)
        v.tensor_scalar(s1[0:1, :], arsb[0:1, :], 0.6, None, ALU.mult)
        v.tensor_scalar(tmp[0:1, 0:1], gf[0:1, :], 1.4, -1.0, ALU.mult, ALU.add)
        v.tensor_tensor(tmp[0:1, 1:2], tmp[0:1, 0:1], arsb[0:1, :], ALU.mult)
        v.tensor_scalar(s0[0:1, :], tmp[0:1, 1:2], 1.0, 0.1, ALU.mult, ALU.add)
    ctx.vec(emitv2)
    ctx.barrier()
    es2.close()


def emit_coupling_cycle(ctx, src, src_f32, dst, cw, s0, s1, CBsb):
    es2 = ExitStack()
    yt = ctx.sbuf(es2, "cc_y", [128, NCH, TCH], F32 if src_f32 else BF16)
    yb = ctx.sbuf(es2, "cc_yb", [128, NCH, TCH], BF16)
    h1 = ctx.sbuf(es2, "cc_h1", [128, 8, TCH], BF16)
    h2 = ctx.sbuf(es2, "cc_h2", [128, 4, TCH], BF16)
    h3 = ctx.sbuf(es2, "cc_h3", [128, 2, TCH], BF16)
    tfr = ctx.sbuf(es2, "cc_tf", [1, T], F32)
    xa = ctx.sbuf(es2, "cc_xa", [128, T], F32 if src_f32 else BF16)
    xb2 = ctx.sbuf(es2, "cc_xb", [128, T], F32 if src_f32 else BF16)
    xo = ctx.sbuf(es2, "cc_xo", [128, T], F32)
    srcv = fm(src)
    dstv = fm(dst)
    w1sb, w2sb, w3sb, w4sb, b1sb, b2sb, b3sb, b4sb = cw

    for tc in range(NTCH):
        ldv = 0
        for c in range(NCH):
            ldv = load_tile(ctx, yt[:, c, :], srcv[:, c, tc * TCH:(tc + 1) * TCH])

        if src_f32:
            def emitcast(v, ldv=ldv):
                wge(v, ldv)
                v.tensor_copy(yb[:], yt[:])
            ctx.vec(emitcast)
            yuse = yb
            vv = ctx.cnt["vself"]
        else:
            yuse = yt
            vv = 0

        for half in range(2):
            avw = ctx.cnt["aself"] if half == 1 else 0

            def emitt(t, half=half, ldv=ldv, vv=vv, avw=avw):
                if half == 0:
                    if vv > 0:
                        t.wait_ge(ctx.sem("vself"), vv)
                    else:
                        wge(t, ldv)
                else:
                    t.wait_ge(ctx.sem("aself"), avw)
                for m in range(4 * half, 4 * half + 4):
                    for c in range(NCH):
                        t.matmul(ctx.psum[:, m % 4, :], w1sb[:, c, m * 128:(m + 1) * 128],
                                 yuse[:, c, :], start=(c == 0), stop=(c == NCH - 1))
                t.drain().then_inc(ctx.sem("t"), 1)
            ctx.tns(emitt)
            tv1 = ctx.inc("t")

            def emita(s, half=half, tv1=tv1):
                s.wait_ge(ctx.sem("t"), tv1)
                for m in range(4 * half, 4 * half + 4):
                    s.activation(h1[:, m, :], ctx.psum[:, m % 4, :], AF.Gelu, bias=b1sb[:, m:m + 1])
            ctx.scl(emita)
        av = ctx.cnt["aself"]

        def emitt2(t, av=av):
            t.wait_ge(ctx.sem("aself"), av)
            for m in range(4):
                for c in range(8):
                    t.matmul(ctx.psum[:, 4 + m, :], w2sb[:, c, m * 128:(m + 1) * 128],
                             h1[:, c, :], start=(c == 0), stop=(c == 7))
            t.drain().then_inc(ctx.sem("t"), 1)
        ctx.tns(emitt2)
        tv = ctx.inc("t")

        def emita2(s, tv=tv):
            s.wait_ge(ctx.sem("t"), tv)
            for m in range(4):
                s.activation(h2[:, m, :], ctx.psum[:, 4 + m, :], AF.Gelu, bias=b2sb[:, m:m + 1])
        ctx.scl(emita2)
        av = ctx.cnt["aself"]

        def emitt3(t, av=av):
            t.wait_ge(ctx.sem("aself"), av)
            for m in range(2):
                for c in range(4):
                    t.matmul(ctx.psum[:, m, :], w3sb[:, c, m * 128:(m + 1) * 128],
                             h2[:, c, :], start=(c == 0), stop=(c == 3))
            t.drain().then_inc(ctx.sem("t"), 1)
        ctx.tns(emitt3)
        tv = ctx.inc("t")

        def emita3(s, tv=tv):
            s.wait_ge(ctx.sem("t"), tv)
            for m in range(2):
                s.activation(h3[:, m, :], ctx.psum[:, m, :], AF.Gelu, bias=b3sb[:, m:m + 1])
        ctx.scl(emita3)
        av = ctx.cnt["aself"]

        def emitt4(t, av=av):
            t.wait_ge(ctx.sem("aself"), av)
            for c in range(2):
                t.matmul(ctx.psum[0:1, 2, :], w4sb[:, c:c + 1], h3[:, c, :],
                         start=(c == 0), stop=(c == 1))
            t.drain().then_inc(ctx.sem("t"), 1)
        ctx.tns(emitt4)
        tv = ctx.inc("t")

        def emita4(s, tc=tc, tv=tv):
            s.wait_ge(ctx.sem("t"), tv)
            s.activation(tfr[0:1, tc * TCH:(tc + 1) * TCH], ctx.psum[0:1, 2, :],
                         AF.Sigmoid, bias=b4sb[0:1, :])
        ctx.scl(emita4)
        ctx.barrier(["sync", "vector", "scalar", "tensor"])

    def emitc(v):
        v.tensor_scalar(tfr[0:1, :], tfr[0:1, :], s1[0:1, 0:1], s0[0:1, 0:1],
                        ALU.mult, ALU.add)
    ctx.vec(emitc)
    vv = ctx.cnt["vself"]

    for tc in range(NTCH):
        def emitbc(t, tc=tc, vv=vv):
            if tc == 0:
                t.wait_ge(ctx.sem("vself"), vv)
            t.matmul(ctx.psum[:, 3, :], ctx.ones1_f[:], tfr[0:1, tc * TCH:(tc + 1) * TCH],
                     start=True, stop=True)
            t.drain().then_inc(ctx.sem("t"), 1)
        ctx.tns(emitbc)
        tv = ctx.inc("t")

        def emitbc2(v, tc=tc, tv=tv):
            v.wait_ge(ctx.sem("t"), tv)
            v.tensor_copy(CBsb[:, tc * TCH:(tc + 1) * TCH], ctx.psum[:, 3, :])
        ctx.vec(emitbc2)
        ctx.barrier(["vector", "tensor"])

    for c in range(NCH):
        sc = (c + 4) % 8
        sign = ALU.add if c in (0, 1, 6, 7) else ALU.subtract
        ldv = load_tile(ctx, xa[:], srcv[:, c, :])
        ldv = load_tile(ctx, xb2[:], srcv[:, sc, :])

        def emitv(v, ldv=ldv, sign=sign):
            wge(v, ldv)
            v.tensor_tensor(xo[:], CBsb[:], xb2[:], ALU.mult)
            v.tensor_tensor(xo[:], xa[:], xo[:], sign)
        ctx.vec(emitv)
        store_tile(ctx, dstv[:, c, :], xo[:], "vself", ctx.cnt["vself"])
        ctx.barrier(["sync", "vector"])
    ctx.barrier()
    es2.close()


def emit_stage2_body(ctx, nc, xin, xout, w, uid):
    es2 = ExitStack()
    dram = lambda n, s, dt=BF16: _dram(nc, uid + n, s, dt)
    Y = dram("Y", [D, T])
    OA, OB = dram("OA", [D, T], F32), dram("OB", [D, T], F32)
    gf = ctx.sbuf(es2, "gf", [1, 1], F32)
    s0 = ctx.sbuf(es2, "s0", [1, 1], F32)
    s1 = ctx.sbuf(es2, "s1", [1, 1], F32)
    arsb = ctx.sbuf(es2, "arsb", [1, 1], F32)
    CBsb = ctx.sbuf(es2, "CBsb", [128, T], F32)
    load_tile(ctx, arsb[0:1, :], w["adaptive_range"].rearrange("(a b) -> a b", a=1))
    emit_ln(ctx, xin, Y)
    emit_gcnet(ctx, Y, w, gf, s0, s1, arsb)
    w1sb = ctx.sbuf(es2, "cn_w1sb", [128, 8, D], BF16)
    w2sb = ctx.sbuf(es2, "cn_w2sb", [128, 8, 512], BF16)
    w3sb = ctx.sbuf(es2, "cn_w3sb", [128, 4, 256], BF16)
    w4sb = ctx.sbuf(es2, "cn_w4sb", [128, 2], BF16)
    b1sb = ctx.sbuf(es2, "cn_b1sb", [128, 8], F32)
    b2sb = ctx.sbuf(es2, "cn_b2sb", [128, 4], F32)
    b3sb = ctx.sbuf(es2, "cn_b3sb", [128, 2], F32)
    b4sb = ctx.sbuf(es2, "cn_b4sb", [1, 1], F32)
    load_weight(ctx, w["cn_w1"], w1sb, D, D)
    load_weight(ctx, w["cn_w2"], w2sb, D, 512)
    load_weight(ctx, w["cn_w3"], w3sb, 512, 256)
    load_tile(ctx, w4sb[:, :], vrow(w["cn_w4"].rearrange("a b -> (a b)")), slow=True, eng="gpsimd")
    load_tile(ctx, b1sb[:, :], vrow(w["cn_b1"]), slow=True)
    load_tile(ctx, b2sb[:, :], vrow(w["cn_b2"]), slow=True)
    load_tile(ctx, b3sb[:, :], vrow(w["cn_b3"]), slow=True)
    load_tile(ctx, b4sb[0:1, :], w["cn_b4"].rearrange("(a b) -> a b", a=1))
    ctx.barrier()
    cw = (w1sb, w2sb, w3sb, w4sb, b1sb, b2sb, b3sb, b4sb)
    emit_coupling_cycle(ctx, Y, False, OA, cw, s0, s1, CBsb)
    emit_coupling_cycle(ctx, OA, True, OB, cw, s0, s1, CBsb)
    emit_coupling_cycle(ctx, OB, True, OA, cw, s0, s1, CBsb)
    emit_add3f(ctx, xout, xin, OA)
    es2.close()


def emit_add3f(ctx, dst_f32, a_f32, b_f32):
    es2 = ExitStack()
    xa = ctx.sbuf(es2, "af_a", [128, T], F32)
    xb = ctx.sbuf(es2, "af_b", [128, T], F32)
    av_, bv_, dv_ = fm(a_f32), fm(b_f32), fm(dst_f32)
    for c in range(NCH):
        load_tile(ctx, xa[:], av_[:, c, :])
        ldv = load_tile(ctx, xb[:], bv_[:, c, :])

        def emitv(v, ldv=ldv):
            wge(v, ldv)
            v.tensor_tensor(xa[:], xa[:], xb[:], ALU.add)
        ctx.vec(emitv)
        store_tile(ctx, dv_[:, c, :], xa[:], "vself", ctx.cnt["vself"])
        ctx.barrier(["sync", "vector"])
    ctx.barrier()
    es2.close()


def stage2_weights(inp):
    return {k: np.ascontiguousarray(inp["L0_ffn_" + k]) for k in
            ["cn_w1", "cn_b1", "cn_w2", "cn_b2", "cn_w3", "cn_b3", "cn_w4", "cn_b4",
             "gc_w1", "gc_b1", "gc_w2", "gc_b2", "gc_w3", "gc_b3", "adaptive_range"]}




def emit_diag_masks(ctx, es):
    m4 = ctx.sbuf(es, "masks4", [128, 4, TCH], BF16)

    def emitg(g):
        g.memset(m4[:], 0.0)
        for r in range(4):
            g.affine_select(out=m4[:, r, :], in_=m4[:, r, :],
                            compare_op=ALU.is_gt, fill=1.0, base=128 * r,
                            pattern=[[-1, TCH]], channel_multiplier=1)
    ctx.gps(emitg)
    ctx.barrier(["gpsimd", "vector", "tensor", "scalar"])
    return m4


def emit_bounds(ctx, QF, KF, Bcols, nh=16, kv_single=False):
    es2 = ExitStack()
    xt = ctx.sbuf(es2, "bn_x", [128, TCH], BF16)
    sqt = ctx.sbuf(es2, "bn_sq", [128, TCH], BF16)
    pairsel = ctx.sbuf(es2, "bn_ps", [128, 2], BF16)
    qn = ctx.sbuf(es2, "bn_qn", [2, 8], F32)
    kn = ctx.sbuf(es2, "bn_kn", [2, 8], F32)
    col = ctx.sbuf(es2, "bn_col", [2, 4], F32)
    nbf = ctx.sbuf(es2, "bn_nbf", [1, 16], F32)

    def emitg(g):
        g.memset(pairsel[:], 0.0)
        g.memset(pairsel[0:64, 0:1], 1.0)
        g.memset(pairsel[64:128, 1:2], 1.0)
    ctx.gps(emitg)

    def emitz(v):
        v.memset(qn[:], 0.0)
        v.memset(kn[:], 0.0)
    ctx.vec(emitz)
    ctx.barrier(["gpsimd", "vector", "sync", "scalar", "tensor"])

    for which, buf, acc in [(0, QF, qn), (1, KF, kn)]:
        nchw = 8 if not (kv_single and which == 1) else 1
        bufv = buf.rearrange("(c p) t -> p c t", p=128) if nchw == 8 else None
        for c in range(nchw):
            for tc in range(NTCH):
                if nchw == 8:
                    ldv = load_tile(ctx, xt[:], bufv[:, c, tc * TCH:(tc + 1) * TCH])
                    npart = 128
                else:
                    ldv = load_tile(ctx, xt[0:64, :], buf[:, tc * TCH:(tc + 1) * TCH])
                    npart = 64

                def emita(s, ldv=ldv, npart=npart):
                    wge(s, ldv)
                    s.activation(sqt[0:npart, :], xt[0:npart, :], AF.Square)
                ctx.scl(emita)
                av = ctx.cnt["aself"]

                def emitt(t, av=av, npart=npart):
                    t.wait_ge(ctx.sem("aself"), av)
                    t.matmul(ctx.psum[0:2, 0, :], pairsel[0:npart, :], sqt[0:npart, :],
                             start=True, stop=True)
                    t.drain().then_inc(ctx.sem("t"), 1)
                ctx.tns(emitt)
                tv = ctx.inc("t")

                def emitv(v, c=c, tv=tv, acc=acc):
                    v.wait_ge(ctx.sem("t"), tv)
                    v.reduce_max(col[0:2, 0:1], ctx.psum[0:2, 0, :], axis=mybir.AxisListType.X)
                    v.tensor_tensor(acc[0:2, c:c + 1], acc[0:2, c:c + 1], col[0:2, 0:1], ALU.max)
                ctx.vec(emitv)
                ctx.barrier(["sync", "vector", "scalar", "tensor"])
    if kv_single:
        dma_issue(ctx, "ld", "sync", kn[1:2, 0:1], kn[0:1, 0:1])
        ctx.barrier(["sync", "vector", "scalar"])

    def emits(s):
        s.activation(qn[0:2, :], qn[0:2, :], AF.Sqrt)
        s.activation(kn[0:2, :], kn[0:2, :], AF.Sqrt)
    ctx.scl(emits)
    av = ctx.cnt["aself"]

    def emitv2(v, av=av):
        v.wait_ge(ctx.sem("aself"), av)
        if kv_single:
            v.tensor_scalar(qn[0:2, :], qn[0:2, :], kn[0:2, 0:1], None, ALU.mult)
        else:
            v.tensor_tensor(qn[0:2, :], qn[0:2, :], kn[0:2, :], ALU.mult)
        v.tensor_scalar(qn[0:2, :], qn[0:2, :], -1.0, None, ALU.mult)
    ctx.vec(emitv2)
    ctx.barrier(["sync", "vector"])
    dma_issue(ctx, "ld", "sync", nbf[0:1, 0:8], qn[0:1, :])
    dma_issue(ctx, "ld", "sync", nbf[0:1, 8:16], qn[1:2, :])
    ldv = dma_mark(ctx, "ld")

    def emitt2(t, ldv=ldv):
        wge(t, ldv)
        t.matmul(ctx.psum[:, 0, 0:16], ctx.ones1_f[:], nbf[0:1, :], start=True, stop=True)
        t.drain().then_inc(ctx.sem("t"), 1)
    ctx.tns(emitt2)
    tv = ctx.inc("t")

    def emitv3(v, tv=tv):
        v.wait_ge(ctx.sem("t"), tv)
        v.tensor_copy(Bcols[:, :], ctx.psum[:, 0, 0:16])
    ctx.vec(emitv3)
    ctx.barrier()
    es2.close()


def bflat(h):
    return (h % 2) * 8 + h // 2


def emit_attn_bound(ctx, QF, KF, VT, OF, Bcols, m4, nh=16, kv_single=False):
    es2 = ExitStack()
    ksb = ctx.sbuf(es2, "at_k", [128, 8 if not kv_single else 1, T], BF16)
    vsb = ctx.sbuf(es2, "at_v", [128, 32, nh if not kv_single else 1, 65], BF16)
    qsb = ctx.sbuf(es2, "at_q", [128, TCH], BF16)
    atsb = ctx.sbuf(es2, "at_a", [128, TCH], BF16)
    rb = ctx.sbuf(es2, "at_r", [128, TCH], F32)
    rbb = ctx.sbuf(es2, "at_rb", [64, TCH], F32)
    osb = ctx.sbuf(es2, "at_o", [64, TCH], BF16)
    nkv = 1 if kv_single else nh
    if kv_single:
        load_tile(ctx, ksb[0:64, 0, :], KF[:, :])
        load_tile(ctx, ksb[64:128, 0, :], KF[:, :])
    else:
        kv_ = fm(KF)
        for c in range(NCH):
            load_tile(ctx, ksb[:, c, :], kv_[:, c, :])
    def emitg(g):
        g.memset(vsb[:], 1.0)
    ctx.gps(emitg)
    ctx.barrier(["gpsimd", "sync"])
    for j in range(32):
        if kv_single:
            load_tile(ctx, vsb[:, j, 0, 0:64], VT[j * 128:(j + 1) * 128, :])
        else:
            load_tile(ctx, vsb[:, j, :, 0:64],
                      VT[j * 128:(j + 1) * 128, :].rearrange("p (h d) -> p h d", d=64))
    ctx.barrier()

    ofv = fm(OF)
    for h in range(nh):
        hp = (h % 2) * 64
        cd = h // 2
        hk = 0 if kv_single else cd
        hkp = 0 if kv_single else hp
        hv = 0 if kv_single else h
        for qc in range(NTCH):
            ldv = load_tile(ctx, qsb[hp:hp + 64, :],
                            QF[h * 64:(h + 1) * 64, qc * TCH:(qc + 1) * TCH])
            njs = 4 * qc + 4
            for j in range(njs):
                sbank = j % 2

                def emit_sc(t, h=h, hp=hp, hkp=hkp, hk=hk, j=j, sbank=sbank, ldv=ldv):
                    if j == 0:
                        wge(t, ldv)
                    if j >= 2:
                        t.wait_ge(ctx.sem("aself"), ctx.cnt["aself"] - 1)
                    kb = hkp if not kv_single else hp
                    t.matmul(ctx.psum[:, sbank, :], ksb[kb:kb + 64, hk, j * 128:(j + 1) * 128],
                             qsb[hp:hp + 64, :], start=True, stop=True,
                             tile_position=(hp, 0))
                    t.drain().then_inc(ctx.sem("t"), 1)
                ctx.tns(emit_sc)
                tv = ctx.inc("t")
                diag = (j >= 4 * qc)

                t2w = ctx.cnt["t2"] if j >= 1 else 0

                def emit_exp(s, h=h, j=j, sbank=sbank, tv=tv, t2w=t2w):
                    s.wait_ge(ctx.sem("t"), tv)
                    if t2w > 0:
                        s.wait_ge(ctx.sem("t2"), t2w)
                    s.activation(atsb[:], ctx.psum[:, sbank, :], AF.Exp,
                                 bias=Bcols[:, bflat(h):bflat(h) + 1])
                ctx.scl(emit_exp)
                av = ctx.cnt["aself"]

                if diag:
                    r = j - 4 * qc

                    def emit_mask(v, r=r, av=av):
                        v.wait_ge(ctx.sem("aself"), av)
                        v.tensor_tensor(atsb[:], atsb[:], m4[:, r, :], ALU.mult)
                    ctx.vec(emit_mask)
                    consume = ("vself", ctx.cnt["vself"])
                else:
                    consume = ("aself", av)

                def emit_ov(t, h=h, hv=hv, j=j, consume=consume, last=(j == njs - 1)):
                    t.wait_ge(ctx.sem(consume[0]), consume[1])
                    t.matmul(ctx.psum[0:65, 3, :], vsb[:, j, hv, :], atsb[:],
                             start=(j == 0), stop=last)
                    t.drain().then_inc(ctx.sem("t2"), 1)
                ctx.tns(emit_ov)
                ctx.inc("t2")

            t2v = ctx.cnt["t2"]

            def emit_fin(v, t2v=t2v):
                v.wait_ge(ctx.sem("t2"), t2v)
                v.reciprocal(rb[64:65, :], ctx.psum[64:65, 3, :])
            ctx.vec(emit_fin)
            vv = ctx.cnt["vself"]

            def emit_bc(t, vv=vv):
                t.wait_ge(ctx.sem("vself"), vv)
                t.matmul(ctx.psum[0:64, 2, :], ctx.ones2d[64:65, :], rb[64:65, :],
                         start=True, stop=True, tile_position=(64, 0))
                t.drain().then_inc(ctx.sem("t"), 1)
            ctx.tns(emit_bc)
            tv = ctx.inc("t")

            def emit_fin2(v, tv=tv):
                v.wait_ge(ctx.sem("t"), tv)
                v.tensor_copy(rbb[:, :], ctx.psum[0:64, 2, :])
                v.tensor_tensor(osb[:, :], ctx.psum[0:64, 3, :], rbb[:, :], ALU.mult)
            ctx.vec(emit_fin2)
            store_tile(ctx, ofv[hp:hp + 64, cd, qc * TCH:(qc + 1) * TCH], osb[:],
                       "vself", ctx.cnt["vself"])
            ctx.barrier(["sync", "vector", "scalar", "tensor"])
    ctx.barrier()
    es2.close()



def emit_stage3_body(ctx, nc, xin, xout, w, uid):
    es2 = ExitStack()
    dram = lambda n, s, dt=BF16: _dram(nc, uid + n, s, dt)
    Y = dram("Y", [D, T])
    QF, KF = dram("QF", [D, T]), dram("KF", [D, T])
    VT, OF = dram("VT", [T, D]), dram("OF", [D, T])
    m4 = emit_diag_masks(ctx, es2)
    Bcols = ctx.sbuf(es2, "Bcols", [128, 16], F32)
    emit_ln(ctx, xin, Y)
    emit_proj_fm(ctx, w["wq"], D, D, Y, QF, scale=0.125)
    emit_proj_fm(ctx, w["wk"], D, D, Y, KF)
    emit_proj_tm(ctx, w["wv"], D, D, Y, VT)
    emit_bounds(ctx, QF, KF, Bcols)
    emit_attn_bound(ctx, QF, KF, VT, OF, Bcols, m4)
    emit_proj_fm(ctx, w["wo"], D, D, OF, None, res=xin, dst_f32=xout)
    es2.close()


def stage3_weights(inp):
    return {
        "wq": np.ascontiguousarray(inp["L0_attn_b_q_proj"].T),
        "wk": np.ascontiguousarray(inp["L0_attn_b_k_proj"].T),
        "wv": np.ascontiguousarray(inp["L0_attn_b_v_proj"].T),
        "wo": np.ascontiguousarray(inp["L0_attn_b_o_proj"].T),
    }




def emit_rope_combine(ctx, QF, QS, CT, ST, QR):
    es2 = ExitStack()
    qa = ctx.sbuf(es2, "rp_a", [128, T], BF16)
    qs = ctx.sbuf(es2, "rp_s", [128, T], BF16)
    ct = ctx.sbuf(es2, "rp_c", [128, T], F32)
    st_ = ctx.sbuf(es2, "rp_t", [128, T], F32)
    qo = ctx.sbuf(es2, "rp_o", [128, T], BF16)
    tmp = ctx.sbuf(es2, "rp_x", [128, T], F32)
    qv, qsv, qrv = fm(QF), fm(QS), fm(QR)
    ctv, stv = fm(CT), fm(ST)
    for c in range(NCH):
        load_tile(ctx, qa[:], qv[:, c, :])
        load_tile(ctx, qs[:], qsv[:, c, :])
        load_tile(ctx, ct[:], ctv[:, c, :])
        ldv = load_tile(ctx, st_[:], stv[:, c, :])

        def emitv(v, ldv=ldv):
            wge(v, ldv)
            v.tensor_tensor(tmp[:], qa[:], ct[:], ALU.mult)
            v.tensor_tensor(qo[:], qs[:], st_[:], ALU.mult)
            v.tensor_tensor(qo[:], tmp[:], qo[:], ALU.add)
        ctx.vec(emitv)
        store_tile(ctx, qrv[:, c, :], qo[:], "vself", ctx.cnt["vself"])
        ctx.barrier(["sync", "vector"])
    ctx.barrier()
    es2.close()


def emit_stage7_body(ctx, nc, xin, xout, w, uid):
    es2 = ExitStack()
    dram = lambda n, s, dt=BF16: _dram(nc, uid + n, s, dt)
    Y = dram("Y", [D, T])
    QF, QS, KF, KS = dram("QF", [D, T]), dram("QS", [D, T]), dram("KF", [D, T]), dram("KS", [D, T])
    QR, KR = dram("QR", [D, T]), dram("KR", [D, T])
    VT, OF = dram("VT", [T, D]), dram("OF", [D, T])
    m4 = emit_diag_masks(ctx, es2)
    Bcols = ctx.sbuf(es2, "Bcols", [128, 16], F32)
    emit_ln(ctx, xin, Y)
    emit_proj_fm(ctx, w["wq"], D, D, Y, QF, scale=0.125)
    emit_proj_fm(ctx, w["wqs"], D, D, Y, QS, scale=0.125)
    emit_proj_fm(ctx, w["wk"], D, D, Y, KF)
    emit_proj_fm(ctx, w["wks"], D, D, Y, KS)
    emit_proj_tm(ctx, w["wv"], D, D, Y, VT)
    emit_rope_combine(ctx, QF, QS, w["ropec"], w["ropesq"], QR)
    emit_rope_combine(ctx, KF, KS, w["ropec"], w["ropesq"], KR)
    emit_bounds(ctx, QR, KR, Bcols)
    emit_attn_bound(ctx, QR, KR, VT, OF, Bcols, m4)
    emit_proj_fm(ctx, w["wo"], D, D, OF, None, res=xin, dst_f32=xout)
    es2.close()


def stage7_weights(inp):
    wqkv = np.asarray(inp["L1_attn_b_w_qkv"])
    wq = wqkv[:, 0:D]
    wk = wqkv[:, D:2 * D]
    wv = wqkv[:, 2 * D:3 * D]
    perm = np.arange(D).reshape(16, 2, 32)[:, ::-1, :].reshape(-1)
    wqs = wq[:, perm]
    wks = wk[:, perm]
    dh = 64
    inv_freq = 1.0 / (10000.0 ** (np.arange(0, dh, 2, dtype=np.float32) / dh))
    t = np.arange(T, dtype=np.float32)
    freqs = t[:, None] * inv_freq[None, :]
    cos, sin = np.cos(freqs), np.sin(freqs)
    crep = np.tile(cos.T, (32, 1)).astype(np.float32)
    srep = np.tile(sin.T, (32, 1)).astype(np.float32)
    sgn = np.where((np.arange(D) % 64) < 32, -1.0, 1.0).astype(np.float32)[:, None]
    sq = srep * sgn
    return {
        "wq": np.ascontiguousarray(wq), "wqs": np.ascontiguousarray(wqs),
        "wk": np.ascontiguousarray(wk), "wks": np.ascontiguousarray(wks),
        "wv": np.ascontiguousarray(wv),
        "wo": np.ascontiguousarray(inp["L1_attn_b_w_out"]),
        "ropec": np.ascontiguousarray(crep), "ropesq": np.ascontiguousarray(sq),
    }




def emit_gate_combine(ctx, Y, gwt, COMB):
    es2 = ExitStack()
    gw = ctx.sbuf(es2, "gt_w", [128, 8, 8], BF16)
    ysb = ctx.sbuf(es2, "gt_y", [128, NCH, TCH], BF16)
    lgb = ctx.sbuf(es2, "gt_lg", [8, T], BF16)
    ltm = ctx.sbuf(es2, "gt_ltm", [128, 8], F32)
    m1 = ctx.sbuf(es2, "gt_m1", [128, 1], F32)
    m2 = ctx.sbuf(es2, "gt_m2", [128, 1], F32)
    eq1 = ctx.sbuf(es2, "gt_e1", [128, 8], F32)
    eq2 = ctx.sbuf(es2, "gt_e2", [128, 8], F32)
    msk = ctx.sbuf(es2, "gt_mk", [128, 8], F32)
    g1 = ctx.sbuf(es2, "gt_g1", [128, 1], F32)
    g2 = ctx.sbuf(es2, "gt_g2", [128, 1], F32)
    ctb = ctx.sbuf(es2, "gt_ct", [128, 8], BF16)
    load_weight(ctx, gwt, gw, D, 8)
    yv = fm(Y)
    ctx.barrier()
    for tc in range(NTCH):
        ldv = 0
        for c in range(NCH):
            ldv = load_tile(ctx, ysb[:, c, :], yv[:, c, tc * TCH:(tc + 1) * TCH])

        def emitt(t, tc=tc, ldv=ldv):
            wge(t, ldv)
            for c in range(NCH):
                t.matmul(ctx.psum[0:8, 0, :], gw[:, c, :], ysb[:, c, :],
                         start=(c == 0), stop=(c == NCH - 1))
            t.drain().then_inc(ctx.sem("t"), 1)
        ctx.tns(emitt)
        tv = ctx.inc("t")

        def emitv(v, tc=tc, tv=tv):
            v.wait_ge(ctx.sem("t"), tv)
            v.tensor_copy(lgb[0:8, tc * TCH:(tc + 1) * TCH], ctx.psum[0:8, 0, :])
        ctx.vec(emitv)
        ctx.barrier(["sync", "vector", "tensor"])

    for tt in range(T // 128):
        def emitt(t, tt=tt):
            t.matmul(ctx.psum[:, 1, 0:8], lgb[0:8, tt * 128:(tt + 1) * 128],
                     ctx.ident_bf[0:8, 0:8], start=True, stop=True)
            t.drain().then_inc(ctx.sem("t"), 1)
        ctx.tns(emitt)
        tv = ctx.inc("t")

        def emitv(v, tv=tv):
            v.wait_ge(ctx.sem("t"), tv)
            v.tensor_copy(ltm[:], ctx.psum[:, 1, 0:8])
            v.reduce_max(m1[:], ltm[:], axis=mybir.AxisListType.X)
            v.tensor_scalar(eq1[:], ltm[:], m1[:, 0:1], None, ALU.is_equal)
            v.scalar_tensor_tensor(msk[:], eq1[:], -1e9, ltm[:], ALU.mult, ALU.add)
            v.reduce_max(m2[:], msk[:], axis=mybir.AxisListType.X)
            v.tensor_scalar(eq2[:], msk[:], m2[:, 0:1], None, ALU.is_equal)
            v.tensor_tensor(g1[:], m1[:], m2[:], ALU.subtract)
        ctx.vec(emitv)
        vv = ctx.cnt["vself"]

        def emita(s, vv=vv):
            s.wait_ge(ctx.sem("vself"), vv)
            s.activation(g1[:], g1[:], AF.Sigmoid)
        ctx.scl(emita)
        av = ctx.cnt["aself"]

        def emitv2(v, av=av):
            v.wait_ge(ctx.sem("aself"), av)
            v.tensor_scalar(g2[:], g1[:], -1.0, 1.0, ALU.mult, ALU.add)
            v.tensor_scalar(eq1[:], eq1[:], g1[:, 0:1], None, ALU.mult)
            v.tensor_scalar(eq2[:], eq2[:], g2[:, 0:1], None, ALU.mult)
            v.tensor_tensor(ctb[:], eq1[:], eq2[:], ALU.add)
        ctx.vec(emitv2)
        vv = ctx.cnt["vself"]

        def emitt2(t, tt=tt, vv=vv):
            t.wait_ge(ctx.sem("vself"), vv)
            t.matmul(ctx.psum[0:8, 2, 0:128], ctb[:], ctx.ident_bf[:],
                     start=True, stop=True)
            t.drain().then_inc(ctx.sem("t"), 1)
        ctx.tns(emitt2)
        tv = ctx.inc("t")

        def emitv3(v, tt=tt, tv=tv):
            v.wait_ge(ctx.sem("t"), tv)
            v.tensor_copy(COMB[0:8, tt * 128:(tt + 1) * 128], ctx.psum[0:8, 2, 0:128])
        ctx.vec(emitv3)
        ctx.barrier(["vector", "scalar", "tensor"])
    ctx.barrier()
    es2.close()


def emit_scale_by_comb(ctx, Y, COMB, e, YSC):
    es2 = ExitStack()
    ysb = ctx.sbuf(es2, "sc_y", [128, NCH, TCH], BF16)
    osb = ctx.sbuf(es2, "sc_o", [128, NCH, TCH], BF16)
    cb = ctx.sbuf(es2, "sc_cb", [128, TCH], F32)
    yv, ov = fm(Y), fm(YSC)
    for tc in range(NTCH):
        ldv = 0
        for c in range(NCH):
            ldv = load_tile(ctx, ysb[:, c, :], yv[:, c, tc * TCH:(tc + 1) * TCH])

        def emitt(t, tc=tc):
            t.matmul(ctx.psum[:, 6, :], ctx.selrows[:, e, :],
                     COMB[0:8, tc * TCH:(tc + 1) * TCH], start=True, stop=True)
            t.drain().then_inc(ctx.sem("t"), 1)
        ctx.tns(emitt)
        tv = ctx.inc("t")

        def emitv(v, tc=tc, tv=tv, ldv=ldv):
            v.wait_ge(ctx.sem("t"), tv)
            wge(v, ldv)
            v.tensor_copy(cb[:], ctx.psum[:, 6, :])
            for c in range(NCH):
                v.tensor_tensor(osb[:, c, :], ysb[:, c, :], cb[:], ALU.mult)
        ctx.vec(emitv)
        vv = ctx.cnt["vself"]

        def emitst(sy, tc=tc, vv=vv):
            pass
        for c in range(NCH):
            store_tile(ctx, ov[:, c, tc * TCH:(tc + 1) * TCH], osb[:, c, :],
                       "vself", vv)
        ctx.barrier(["sync", "vector", "tensor"])
    ctx.barrier()
    es2.close()


def emit_stage6_body(ctx, nc, xin, xout, w, uid):
    es2 = ExitStack()
    dram = lambda n, s, dt=BF16: _dram(nc, uid + n, s, dt)
    Y = dram("Y", [D, T])
    YSC = dram("YSC", [D, T])
    G = dram("G", [4 * D, T])
    OUTACC = dram("OUTACC", [D, T], F32)
    COMB = ctx.sbuf(es2, "COMB", [8, T], F32)
    emit_ln(ctx, xin, Y)
    emit_gate_combine(ctx, Y, w["gwt"], COMB)
    for e in range(8):
        emit_scale_by_comb(ctx, Y, COMB, e, YSC)
        emit_proj_fm(ctx, w["w1t"][e], D, 4 * D, YSC, G, act=AF.Relu)
        emit_proj_fm(ctx, w["w2t"][e], 4 * D, D, G, None,
                     res=(OUTACC if e > 0 else None), dst_f32=OUTACC)
    emit_add3f(ctx, xout, xin, OUTACC)
    es2.close()


def stage6_weights(inp):
    return {
        "gwt": np.ascontiguousarray(np.asarray(inp["L1_ffn_gate_w"]).T),
        "w1t": np.ascontiguousarray(np.asarray(inp["L1_ffn_w1"]).transpose(0, 2, 1)),
        "w2t": np.ascontiguousarray(np.asarray(inp["L1_ffn_w2"]).transpose(0, 2, 1)),
    }




def emit_proj_small(ctx, w_hbm, n_in, n_out, src, dst, dst_f32=False):
    es2 = ExitStack()
    ncin = n_in // 128
    wsb = ctx.sbuf(es2, "ps_w", [128, ncin, n_out], BF16)
    load_weight(ctx, w_hbm, wsb, n_in, n_out)
    ysb = ctx.sbuf(es2, "ps_y", [128, ncin, TCH], BF16)
    osb = ctx.sbuf(es2, "ps_o", [n_out, TCH], F32 if dst_f32 else BF16)
    srcv = fm(src)
    ctx.barrier()
    for tc in range(NTCH):
        ldv = 0
        for c in range(ncin):
            ldv = load_tile(ctx, ysb[:, c, :], srcv[:, c, tc * TCH:(tc + 1) * TCH])

        def emitt(t, ldv=ldv):
            wge(t, ldv)
            for c in range(ncin):
                t.matmul(ctx.psum[0:n_out, 4, :], wsb[:, c, :], ysb[:, c, :],
                         start=(c == 0), stop=(c == ncin - 1))
            t.drain().then_inc(ctx.sem("t"), 1)
        ctx.tns(emitt)
        tv = ctx.inc("t")

        def emitv(v, tv=tv):
            v.wait_ge(ctx.sem("t"), tv)
            v.tensor_copy(osb[:], ctx.psum[0:n_out, 4, :])
        ctx.vec(emitv)
        store_tile(ctx, dst[:, tc * TCH:(tc + 1) * TCH], osb[:], "vself", ctx.cnt["vself"])
        ctx.barrier(["sync", "vector", "tensor"])
    ctx.barrier()
    es2.close()


def emit_conv_silu(ctx, XBCF, conv_w, conv_b, XC):
    es2 = ExitStack()
    NCC = 17
    xt = ctx.sbuf(es2, "cv_x", [128, TCH + 3], BF16)
    acc = ctx.sbuf(es2, "cv_a", [128, TCH], F32)
    ot = ctx.sbuf(es2, "cv_o", [128, TCH], BF16)
    wc = ctx.sbuf(es2, "cv_w", [128, NCC, 4], F32)
    bc = ctx.sbuf(es2, "cv_b", [128, NCC], F32)
    load_tile(ctx, wc[:, :, :], conv_w.rearrange("(c p) k -> p c k", p=128), slow=True)
    load_tile(ctx, bc[:, :], vrow(conv_b), slow=True)
    xv, ov = fm(XBCF), fm(XC)
    ctx.barrier()
    for c in range(NCC):
        for tc in range(NTCH):
            if tc == 0:
                def emitg(g):
                    g.memset(xt[:, 0:3], 0.0)
                ctx.gps(emitg)
                ldv = load_tile(ctx, xt[:, 3:TCH + 3], xv[:, c, 0:TCH])
                gw = ctx.cnt["gself"]
            else:
                ldv = load_tile(ctx, xt[:, 0:TCH + 3],
                                xv[:, c, tc * TCH - 3:(tc + 1) * TCH])
                gw = 0

            def emitv(v, c=c, ldv=ldv, gw=gw):
                wge(v, ldv)
                if gw > 0:
                    v.wait_ge(ctx.sem("gself"), gw)
                v.tensor_scalar(acc[:], xt[:, 0:TCH], wc[:, c, 0:1], None, ALU.mult)
                for k in range(1, 4):
                    v.scalar_tensor_tensor(acc[:], xt[:, k:TCH + k], wc[:, c, k:k + 1],
                                           acc[:], ALU.mult, ALU.add)
            ctx.vec(emitv)
            vv = ctx.cnt["vself"]

            def emita(s, c=c, vv=vv):
                s.wait_ge(ctx.sem("vself"), vv)
                s.activation(ot[:], acc[:], AF.Silu, bias=bc[:, c:c + 1])
            ctx.scl(emita)
            store_tile(ctx, ov[:, c, tc * TCH:(tc + 1) * TCH], ot[:],
                       "aself", ctx.cnt["aself"])
            ctx.barrier(["sync", "gpsimd", "vector", "scalar"])
    ctx.barrier()
    es2.close()


def emit_dt_prep(ctx, DTRAW, dt_bias, A_log, DTL, LDA):
    es2 = ExitStack()
    NH = 32
    dtr = ctx.sbuf(es2, "dp_r", [NH, T], F32)
    a_ = ctx.sbuf(es2, "dp_a", [NH, T], F32)
    e_ = ctx.sbuf(es2, "dp_e", [NH, T], F32)
    m_ = ctx.sbuf(es2, "dp_m", [NH, T], F32)
    dtb = ctx.sbuf(es2, "dp_b", [NH, 1], F32)
    acol = ctx.sbuf(es2, "dp_A", [NH, 1], F32)
    one_ = ctx.constcol(1.0)
    load_tile(ctx, dtr[:], DTRAW[:, :])
    load_tile(ctx, dtb[:], dt_bias.rearrange("(a b) -> a b", b=1))
    ldv = load_tile(ctx, acol[:], A_log.rearrange("(a b) -> a b", b=1))
    ctx.barrier()

    def emita0(s):
        s.activation(acol[:], acol[:], AF.Exp)
    ctx.scl(emita0)
    av = ctx.cnt["aself"]

    def emitv(v, av=av):
        v.wait_ge(ctx.sem("aself"), av)
        v.tensor_scalar(acol[:], acol[:], -1.0, None, ALU.mult)
        v.tensor_scalar(dtr[:], dtr[:], dtb[:, 0:1], None, ALU.add)
        v.tensor_scalar(m_[:], dtr[:], 0.0, None, ALU.max)
        v.tensor_scalar(a_[:], dtr[:], -1.0, None, ALU.mult)
        v.tensor_tensor(a_[:], a_[:], dtr[:], ALU.max)
    ctx.vec(emitv)
    vv = ctx.cnt["vself"]

    def emita(s, vv=vv):
        s.wait_ge(ctx.sem("vself"), vv)
        s.activation(e_[:], a_[:], AF.Exp, scale=-1.0)
        s.activation(e_[:], e_[:], AF.Ln, bias=one_[0:NH, :])
    ctx.scl(emita)
    av = ctx.cnt["aself"]

    def emitv2(v, av=av):
        v.wait_ge(ctx.sem("aself"), av)
        v.tensor_tensor(dtr[:], m_[:], e_[:], ALU.add)
        v.tensor_scalar(a_[:], dtr[:], acol[:, 0:1], None, ALU.mult)
    ctx.vec(emitv2)
    vv = ctx.cnt["vself"]

    def emita2(s, vv=vv):
        s.wait_ge(ctx.sem("vself"), vv)
        s.activation(m_[:], dtr[:], AF.Ln)
    ctx.scl(emita2)
    store_tile(ctx, LDA[:, :], a_[:], "vself", vv)
    store_tile(ctx, DTL[:, :], m_[:], "aself", ctx.cnt["aself"])
    ctx.barrier()
    es2.close()


def emit_mamba_scan(ctx, XC, DTL, LDA, Dcols_hbm, YM, sel32, maskADD):
    es2 = ExitStack()
    CH = 128
    NH = 32
    xst = ctx.sbuf(es2, "ms_xst", [128, 2176], BF16)
    xsf = ctx.sbuf(es2, "ms_xsf", [128, 16, CH], BF16)
    btile = ctx.sbuf(es2, "ms_b", [64, CH], BF16)
    ctile = ctx.sbuf(es2, "ms_c", [64, CH], BF16)
    sldA = ctx.sbuf(es2, "ms_lda", [NH, CH], F32)
    slnd = ctx.sbuf(es2, "ms_lnd", [NH, CH], F32)
    s_ = ctx.sbuf(es2, "ms_s", [NH, CH], F32)
    ones32 = ctx.sbuf(es2, "ms_one", [NH, CH], F32)
    sT = ctx.sbuf(es2, "ms_sT", [128, NH], F32)
    biascol = ctx.sbuf(es2, "ms_bc", [128, NH], F32)
    esL = ctx.sbuf(es2, "ms_esl", [NH, 1], F32)
    esLrow = ctx.sbuf(es2, "ms_eslr", [1, NH], F32)
    esLbc = ctx.sbuf(es2, "ms_eslb", [64, NH], F32)
    msc = ctx.sbuf(es2, "ms_msc", [128, CH], BF16)
    wsb = ctx.sbuf(es2, "ms_w", [128, 2, CH], BF16)
    gsb = ctx.sbuf(es2, "ms_g", [128, 2, CH], BF16)
    estsb = ctx.sbuf(es2, "ms_est", [64, 2, CH], BF16)
    ctl = ctx.sbuf(es2, "ms_ctl", [64, 2, CH], BF16)
    bsc = ctx.sbuf(es2, "ms_bsc", [128, 2, 64], BF16)
    hstf = ctx.sbuf(es2, "ms_hf", [64, NH, 64], F32)
    hstb = ctx.sbuf(es2, "ms_hb", [64, NH, 64], BF16)
    tmps = ctx.sbuf(es2, "ms_tmp", [64, NH * 64], F32)
    ymc = ctx.sbuf(es2, "ms_ymc", [128, 16, CH], BF16)
    dcols = ctx.sbuf(es2, "ms_dc", [128, 16], F32)
    load_tile(ctx, dcols[:, :], Dcols_hbm[:, :])

    def emitz(g):
        g.memset(hstf[:], 0.0)
        g.memset(hstb[:], 0.0)
        g.memset(ones32[:], 1.0)
    ctx.gps(emitz)
    ctx.barrier()

    ymv = fm(YM)

    def o_psum():
        return ctx.psum[:, 3, 0:CH]

    def du_psum(h):
        b = 4 + h // 8
        off = (h % 8) * 64
        return ctx.psum[0:64, b, off:off + 64]

    for c in range(T // CH):
        cols = slice(c * CH, (c + 1) * CH)
        ldt = [dma_issue(ctx, "ld", "sync", xst[:], XC[:, c * CH:(c + 1) * CH],
                         transpose=True)]
        load_tile(ctx, xsf[:], XC[0:2048, cols].rearrange("(a p) t -> p a t", p=128))
        load_tile(ctx, btile[:], XC[2048:2112, cols])
        load_tile(ctx, ctile[:], XC[2112:2176, cols])
        load_tile(ctx, sldA[:], LDA[:, cols])
        ldv = load_tile(ctx, slnd[:], DTL[:, cols])

        def emitv0(v, ldv=ldv, ldt=ldt):
            wge(v, ldv)
            wge(v, ldt)
            v.tensor_tensor_scan(s_[:], ones32[:], sldA[:], 0.0, ALU.mult, ALU.add)
        ctx.vec(emitv0)
        vv0 = ctx.cnt["vself"]

        def emita0(s, vv0=vv0):
            s.wait_ge(ctx.sem("vself"), vv0)
            s.activation(esL[:], s_[:, CH - 1:CH], AF.Exp)
        ctx.scl(emita0)
        av0 = ctx.cnt["aself"]
        dma_issue(ctx, "ld", "sync", esLrow[0:1, :], esL[:, 0:1],
                  wait=[("aself", av0)])
        eslm = dma_mark(ctx, "ld")

        def emitt0(t, vv0=vv0, eslm=eslm):
            t.wait_ge(ctx.sem("vself"), vv0)
            t.transpose(ctx.psum[:, 2, 0:NH], s_[:], ctx.ident_f[0:NH, 0:NH])
            t.drain().then_inc(ctx.sem("t"), 1)
        ctx.tns(emitt0)
        tv0 = ctx.inc("t")

        def emitv1(v, tv0=tv0):
            v.wait_ge(ctx.sem("t"), tv0)
            v.tensor_copy(sT[:], ctx.psum[:, 2, 0:NH])
        ctx.vec(emitv1)
        vv1 = ctx.cnt["vself"]

        def emitt1(t, vv1=vv1, ldv=ldv, eslm=eslm):
            t.wait_ge(ctx.sem("vself"), vv1)
            t.transpose(ctx.psum[:, 2, 32:32 + NH], slnd[:], ctx.ident_f[0:NH, 0:NH])
            t.matmul(ctx.psum[:, 2, 64:64 + CH], btile[:], ctile[:], start=True, stop=True)
            wge(t, eslm)
            t.matmul(ctx.psum[0:64, 2, 192:192 + NH], ctx.ones2d[0:1, :], esLrow[0:1, :],
                     start=True, stop=True)
            t.drain().then_inc(ctx.sem("t"), 1)
        ctx.tns(emitt1)
        tv1 = ctx.inc("t")

        def emitv2(v, tv1=tv1):
            v.wait_ge(ctx.sem("t"), tv1)
            v.tensor_tensor(biascol[:], ctx.psum[:, 2, 32:32 + NH], sT[:], ALU.subtract)
            v.tensor_tensor(msc[:], ctx.psum[:, 2, 64:64 + CH], ctx.maskUT[:], ALU.mult)
            v.tensor_copy(esLbc[:], ctx.psum[0:64, 2, 192:192 + NH])
        ctx.vec(emitv2)
        vv2 = ctx.cnt["vself"]

        for h in range(NH):
            par = h % 2
            wb = par
            t2w = ctx.cnt["t2"] - 1 if h >= 2 else 0
            avw = ctx.cnt["aself"] if h >= 2 else 0

            def emitWt(t, h=h, wb=wb, vv2=vv2, avw=avw):
                if h == 0:
                    t.wait_ge(ctx.sem("vself"), vv2)
                if avw > 0:
                    t.wait_ge(ctx.sem("aself"), avw - 1)
                t.matmul(ctx.psum[:, wb, 0:CH], sel32[:, h, :], s_[:], start=True, stop=False)
                t.matmul(ctx.psum[:, wb, 0:CH], ctx.ident_f[:], maskADD[:], start=False, stop=True)
                t.drain().then_inc(ctx.sem("t"), 1)
            ctx.tns(emitWt)
            tvW = ctx.inc("t")
            vwait = ctx.cnt["vself"] if h >= 2 else 0

            def emitWe(s, h=h, wb=wb, par=par, tvW=tvW, vwait=vwait):
                s.wait_ge(ctx.sem("t"), tvW)
                if vwait > 0:
                    s.wait_ge(ctx.sem("vself"), vwait)
                s.activation(wsb[:, par, :], ctx.psum[:, wb, 0:CH], AF.Exp,
                             bias=biascol[:, h:h + 1])
                s.activation(estsb[:, par, :], ctx.psum[0:64, wb, 0:CH], AF.Exp)
            ctx.scl(emitWe)
            avW = ctx.cnt["aself"]
            t2g = ctx.cnt["t2"] if h >= 1 else 0

            def emitGv(v, h=h, par=par, avW=avW, t2g=t2g):
                v.wait_ge(ctx.sem("aself"), avW)
                if t2g > 0:
                    v.wait_ge(ctx.sem("t2"), t2g)
                v.tensor_tensor(gsb[:, par, :], wsb[:, par, :], msc[:], ALU.mult)
                v.tensor_tensor(ctl[:, par, :], ctile[:], estsb[:, par, :], ALU.mult)
                bcol = wsb[:, par, CH - 1:CH]
                wcolb = bass.AP(bcol.tensor, bcol.offset,
                                [list(x) for x in bcol.ap[:-1]] + [[0, 64]])
                v.tensor_tensor(bsc[:, par, :], xst[:, 2048:2112], wcolb, ALU.mult)
            ctx.vec(emitGv)
            vG = ctx.cnt["vself"]
            hp = (h % 2) * 64

            def emitMM(t, h=h, par=par, vG=vG, hp=hp):
                t.wait_ge(ctx.sem("vself"), vG)
                t.matmul(o_psum()[hp:hp + 64, :], xst[:, h * 64:(h + 1) * 64],
                         gsb[:, par, :], start=True, stop=False, tile_position=(0, hp))
                t.matmul(o_psum()[hp:hp + 64, :], hstb[:, h, :], ctl[:, par, :],
                         start=False, stop=True, tile_position=(0, hp))
                t.matmul(du_psum(h), bsc[:, par, :], xst[:, h * 64:(h + 1) * 64],
                         start=True, stop=True, tile_position=(0, 0))
                t.drain().then_inc(ctx.sem("t2"), 1)
            ctx.tns(emitMM)
            t2v = ctx.inc("t2")

            if h % 2 == 1:
                def emitEv(v, h=h, c=c, t2v=t2v):
                    v.wait_ge(ctx.sem("t2"), t2v)
                    ph = h // 2
                    v.scalar_tensor_tensor(ymc[:, ph, :], xsf[:, ph, :], dcols[:, ph:ph + 1],
                                           o_psum(), ALU.mult, ALU.add)
                ctx.vec(emitEv)

        t2v = ctx.cnt["t2"]
        vv3 = ctx.cnt["vself"]

        def emitSt(v, t2v=t2v):
            v.wait_ge(ctx.sem("t2"), t2v)
            eb = esLbc[:, :]
            ebb = bass.AP(eb.tensor, eb.offset,
                          [list(x) for x in eb.ap] + [[0, 64]])
            v.tensor_tensor(tmps.rearrange("p (a b) -> p a b", b=64), hstf[:], ebb, ALU.mult)
            v.tensor_tensor(hstf[:, 0:8, :].rearrange("p a b -> p (a b)"),
                            tmps[:, 0:512], ctx.psum[0:64, 4, :], ALU.add)
            v.tensor_tensor(hstf[:, 8:16, :].rearrange("p a b -> p (a b)"),
                            tmps[:, 512:1024], ctx.psum[0:64, 5, :], ALU.add)
            v.tensor_tensor(hstf[:, 16:24, :].rearrange("p a b -> p (a b)"),
                            tmps[:, 1024:1536], ctx.psum[0:64, 6, :], ALU.add)
            v.tensor_tensor(hstf[:, 24:32, :].rearrange("p a b -> p (a b)"),
                            tmps[:, 1536:2048], ctx.psum[0:64, 7, :], ALU.add)
            v.tensor_copy(hstb[:], hstf[:])
        ctx.vec(emitSt)
        store_tile(ctx, ymv[:, :, cols], ymc[:], "vself", ctx.cnt["vself"])
        ctx.barrier(["sync", "gpsimd", "vector", "scalar", "tensor"])
    ctx.barrier()
    es2.close()


def emit_gate_silu(ctx, YM, ZF, GM):
    es2 = ExitStack()
    ya = ctx.sbuf(es2, "gs_y", [128, T], BF16)
    za = ctx.sbuf(es2, "gs_z", [128, T], BF16)
    oa = ctx.sbuf(es2, "gs_o", [128, T], BF16)
    yv = YM.rearrange("(c p) t -> p c t", p=128)
    zv = ZF.rearrange("(c p) t -> p c t", p=128)
    ov = GM.rearrange("(c p) t -> p c t", p=128)
    for c in range(16):
        load_tile(ctx, ya[:], yv[:, c, :])
        ldv = load_tile(ctx, za[:], zv[:, c, :])

        def emita(s, ldv=ldv):
            wge(s, ldv)
            s.activation(za[:], za[:], AF.Silu)
        ctx.scl(emita)
        av = ctx.cnt["aself"]

        def emitv(v, av=av):
            v.wait_ge(ctx.sem("aself"), av)
            v.tensor_tensor(oa[:], ya[:], za[:], ALU.mult)
        ctx.vec(emitv)
        store_tile(ctx, ov[:, c, :], oa[:], "vself", ctx.cnt["vself"])
        ctx.barrier(["sync", "vector", "scalar"])
    ctx.barrier()
    es2.close()


def emit_shift_mix(ctx, R2, maa_k, maa_r, XK, XR):
    es2 = ExitStack()
    ra = ctx.sbuf(es2, "sm_r", [128, T], BF16)
    rp = ctx.sbuf(es2, "sm_p", [128, T], BF16)
    dd = ctx.sbuf(es2, "sm_d", [128, T], F32)
    ok = ctx.sbuf(es2, "sm_k", [128, T], BF16)
    orr = ctx.sbuf(es2, "sm_rr", [128, T], BF16)
    mk = ctx.sbuf(es2, "sm_mk", [128, 2, NCH], F32)
    rv, kv_, rv2 = fm(R2), fm(XK), fm(XR)
    load_tile(ctx, mk[:, 0, :], vrow(maa_k), slow=True)
    load_tile(ctx, mk[:, 1, :], vrow(maa_r), slow=True)
    ctx.barrier()
    for c in range(NCH):
        def emitg(g):
            g.memset(rp[:, 0:1], 0.0)
        ctx.gps(emitg)
        gw = ctx.cnt["gself"]
        load_tile(ctx, ra[:], rv[:, c, :])
        ldv = load_tile(ctx, rp[:, 1:T], rv[:, c, 0:T - 1])

        def emitv(v, c=c, ldv=ldv, gw=gw):
            wge(v, ldv)
            v.wait_ge(ctx.sem("gself"), gw)
            v.tensor_tensor(dd[:], rp[:], ra[:], ALU.subtract)
            v.scalar_tensor_tensor(ok[:], dd[:], mk[:, 0, c:c + 1], ra[:], ALU.mult, ALU.add)
            v.scalar_tensor_tensor(orr[:], dd[:], mk[:, 1, c:c + 1], ra[:], ALU.mult, ALU.add)
        ctx.vec(emitv)
        store_tile(ctx, kv_[:, c, :], ok[:], "vself", ctx.cnt["vself"])
        store_tile(ctx, rv2[:, c, :], orr[:], "vself", ctx.cnt["vself"])
        ctx.barrier(["sync", "gpsimd", "vector"])
    ctx.barrier()
    es2.close()


def emit_final_cmix(ctx, xout, xin, V2, RH, VH):
    es2 = ExitStack()
    xa = ctx.sbuf(es2, "fc_x", [128, T], F32)
    v2 = ctx.sbuf(es2, "fc_v", [128, T], F32)
    rh = ctx.sbuf(es2, "fc_r", [128, T], BF16)
    vh = ctx.sbuf(es2, "fc_h", [128, T], BF16)
    tt_ = ctx.sbuf(es2, "fc_t", [128, T], F32)
    xv, vv_, rv, hv, ov = fm(xin), fm(V2), fm(RH), fm(VH), fm(xout)
    for c in range(NCH):
        load_tile(ctx, xa[:], xv[:, c, :])
        load_tile(ctx, v2[:], vv_[:, c, :])
        load_tile(ctx, rh[:], rv[:, c, :])
        ldv = load_tile(ctx, vh[:], hv[:, c, :])

        def emitv(v, ldv=ldv):
            wge(v, ldv)
            v.tensor_tensor(tt_[:], rh[:], vh[:], ALU.mult)
            v.tensor_tensor(tt_[:], v2[:], tt_[:], ALU.add)
            v.tensor_tensor(tt_[:], xa[:], tt_[:], ALU.add)
        ctx.vec(emitv)
        store_tile(ctx, ov[:, c, :], tt_[:], "vself", ctx.cnt["vself"])
        ctx.barrier(["sync", "vector"])
    ctx.barrier()
    es2.close()


def emit_stage5_body(ctx, nc, xin, xout, w, uid):
    es2 = ExitStack()
    dram = lambda n, s, dt=BF16: _dram(nc, uid + n, s, dt)
    U = dram("U", [D, T])
    MIN_ = dram("MIN", [D, T])
    ZF = dram("ZF", [2048, T])
    XBCF = dram("XBCF", [2176, T])
    XC = dram("XC", [2176, T])
    DTRAW = dram("DTRAW", [32, T], F32)
    DTL = dram("DTL", [32, T], F32)
    LDA = dram("LDA", [32, T], F32)
    YM = dram("YM", [2048, T])
    GM = dram("GM", [2048, T])
    GN = dram("GN", [2048, T])
    V1 = dram("V1", [D, T], F32)
    R1 = dram("R1", [D, T])
    QF = dram("QF", [D, T])
    KS = dram("KS", [64, T])
    VTS = dram("VTS", [T, 64])
    OF = dram("OF", [D, T])
    V2 = dram("V2", [D, T], F32)
    R2 = dram("R2", [D, T])
    XK, XR = dram("XK", [D, T]), dram("XR", [D, T])
    KH = dram("KH", [4 * D, T])
    RH = dram("RH", [D, T])
    VH = dram("VH", [D, T])

    m4 = emit_diag_masks(ctx, es2)
    Bcols = ctx.sbuf(es2, "Bcols", [128, 16], F32)
    sel32 = ctx.sbuf(es2, "sel32", [32, 32, 128], F32)
    maskADD = ctx.sbuf(es2, "maskADD", [128, 128], F32)

    def emitg(g):
        g.memset(sel32[:], 0.0)
        g.affine_select(out=sel32[:], in_=sel32[:], compare_op=ALU.not_equal,
                        fill=1.0, base=0, pattern=[[-1, 32], [0, 128]],
                        channel_multiplier=1)
        g.memset(maskADD[:], -1e9)
        g.affine_select(out=maskADD[:], in_=maskADD[:], compare_op=ALU.is_gt,
                        fill=0.0, base=0, pattern=[[-1, 128]], channel_multiplier=1)
    ctx.gps(emitg)
    ctx.barrier()

    emit_ln(ctx, xin, U)
    emit_ln(ctx, U, MIN_, rms=True, src_bf16=True)
    emit_proj_fm(ctx, w["in_proj_w"][:, 0:2048], D, 2048, MIN_, ZF)
    emit_proj_fm(ctx, w["in_proj_w"][:, 2048:4224], D, 2176, MIN_, XBCF)
    emit_proj_small(ctx, w["in_proj_w"][:, 4224:4256], D, 32, MIN_, DTRAW, dst_f32=True)
    emit_conv_silu(ctx, XBCF, w["conv_w"], w["conv_b"], XC)
    emit_dt_prep(ctx, DTRAW, w["dt_bias"], w["A_log"], DTL, LDA)
    emit_mamba_scan(ctx, XC, DTL, LDA, w["Dcols"], YM, sel32, maskADD)
    emit_gate_silu(ctx, YM, ZF, GM)
    emit_ln(ctx, GM, GN, rms=True, src_bf16=True, gamma=w["mnorm_w"], n_feat=2048)
    emit_proj_fm(ctx, w["out_proj_w"], 2048, D, GN, None, res=U, dst_f32=V1)
    emit_ln(ctx, V1, R1, rms=True)
    emit_proj_fm(ctx, w["attn_w"][:, 0:D], D, D, R1, QF, scale=0.125)
    emit_proj_small(ctx, w["attn_w"][:, D:D + 64], D, 64, R1, KS)
    emit_proj_tm(ctx, w["attn_w"][:, D + 64:D + 128], D, 64, R1, VTS)
    emit_bounds(ctx, QF, KS, Bcols, kv_single=True)
    emit_attn_bound(ctx, QF, KS, VTS, OF, Bcols, m4, kv_single=True)
    emit_proj_fm(ctx, w["proj_w"], D, D, OF, None, res=V1, dst_f32=V2)
    emit_ln(ctx, V2, R2, rms=True)
    emit_shift_mix(ctx, R2, w["time_maa_k"], w["time_maa_r"], XK, XR)
    erf_scale = 1.0 / (np.sqrt(1.0 / (4.0 * np.pi)) * np.sqrt(2.0))
    mu = float(np.sqrt(0.5))
    emit_proj_fm(ctx, w["key_w"], D, 4 * D, XK, KH, act=AF.Erf,
                 scale=float(erf_scale), bias_const=float(-mu * erf_scale),
                 post_affine=(0.5, ctx.constcol(0.5)))
    emit_proj_fm(ctx, w["recept_w"], D, D, XR, RH, act=AF.Sigmoid)
    emit_proj_fm(ctx, w["value_w"], 4 * D, D, KH, VH)
    emit_final_cmix(ctx, xout, xin, V2, RH, VH)
    es2.close()


def stage5_weights(inp):
    g = lambda k: np.ascontiguousarray(np.asarray(inp["L1_attn_a_" + k]))
    Dv = np.asarray(inp["L1_attn_a_D"])
    dcols = np.repeat(Dv.reshape(16, 2).T[None, :, :], 64, axis=0)
    dcols = np.ascontiguousarray(dcols.transpose(1, 0, 2).reshape(128, 16).astype(np.float32))
    return {
        "in_proj_w": g("in_proj_w"), "conv_w": g("conv_w"), "conv_b": g("conv_b"),
        "dt_bias": g("dt_bias"), "A_log": g("A_log"), "Dcols": dcols,
        "mnorm_w": g("mnorm_w"), "out_proj_w": g("out_proj_w"),
        "attn_w": g("attn_w"), "proj_w": g("proj_w"),
        "time_maa_k": g("time_maa_k"), "time_maa_r": g("time_maa_r"),
        "key_w": g("key_w"), "recept_w": g("recept_w"), "value_w": g("value_w"),
    }



def emit_stage1_body(ctx, nc, xin, xout, w, uid):
    es2 = ExitStack()
    dram = lambda n, s, dt=BF16: _dram(nc, uid + n, s, dt)
    U, Y1 = dram("U", [D, T]), dram("Y1", [D, T])
    QF, KF = dram("QF", [D, T]), dram("KF", [D, T])
    KT, VT = dram("KT", [T, D]), dram("VT", [T, D])
    OF, U1, Y2 = dram("OF", [D, T]), dram("U1", [D, T]), dram("Y2", [D, T])
    U1f = dram("U1f", [D, T], F32)
    HA, HB, G = dram("HA", [D, T]), dram("HB", [D, T]), dram("G", [4 * D, T])
    emit_ln(ctx, xin, U)
    emit_ln(ctx, U, Y1, gamma=w["ln1_g"], beta=w["ln1_b"], src_bf16=True)
    emit_proj_fm(ctx, w["wq"], D, D, Y1, QF)
    emit_proj_fm(ctx, w["wk"], D, D, Y1, KF)
    emit_proj_tm(ctx, w["wk"], D, D, Y1, KT)
    emit_proj_tm(ctx, w["wv"], D, D, Y1, VT)
    emit_linattn(ctx, QF, KF, KT, VT, OF)
    emit_proj_fm(ctx, w["wo"], D, D, OF, None, res=U, dst_f32=U1f)
    emit_cast(ctx, U1f, U1)
    emit_ln(ctx, U1, Y2, gamma=w["ln2_g"], beta=w["ln2_b"], src_bf16=True)
    hsrc = Y2
    for l in range(3):
        hdst = HA if l % 2 == 0 else HB
        emit_proj_fm(ctx, w["cms_w1"][l], D, 4 * D, hsrc, G,
                     bias=w["cms_b1"][l], act=AF.Gelu_apprx_tanh)
        emit_proj_fm(ctx, w["cms_w2"][l], 4 * D, D, G, hdst, bias=w["cms_b2"][l])
        hsrc = hdst
    emit_add3(ctx, xout, xin, U1, hsrc)
    es2.close()




STAGE_INPUTS = {
    "s1": [("ln1_g", [D]), ("ln1_b", [D]), ("wq", [D, D]), ("wk", [D, D]),
           ("wv", [D, D]), ("wo", [D, D]), ("ln2_g", [D]), ("ln2_b", [D]),
           ("cms_w1", [3, D, 4 * D]), ("cms_b1", [3, 4 * D]),
           ("cms_w2", [3, 4 * D, D]), ("cms_b2", [3, D])],
    "s2": [("cn_w1", [D, D]), ("cn_b1", [D]), ("cn_w2", [D, 512]), ("cn_b2", [512]),
           ("cn_w3", [512, 256]), ("cn_b3", [256]), ("cn_w4", [256, 1]), ("cn_b4", [1]),
           ("gc_w1", [D, 512]), ("gc_b1", [512]), ("gc_w2", [512, 256]), ("gc_b2", [256]),
           ("gc_w3", [256, 1]), ("gc_b3", [1]), ("adaptive_range", [1])],
    "s3": [("wq", [D, D]), ("wk", [D, D]), ("wv", [D, D]), ("wo", [D, D])],
    "s5": [("in_proj_w", [D, 4256]), ("conv_w", [2176, 4]), ("conv_b", [2176]),
           ("dt_bias", [32]), ("A_log", [32]), ("Dcols", [128, 16]),
           ("mnorm_w", [2048]), ("out_proj_w", [2048, D]),
           ("attn_w", [D, 1152]), ("proj_w", [D, D]),
           ("time_maa_k", [D]), ("time_maa_r", [D]),
           ("key_w", [D, 4 * D]), ("recept_w", [D, D]), ("value_w", [4 * D, D])],
    "s6": [("gwt", [D, 8]), ("w1t", [8, D, 4 * D]), ("w2t", [8, 4 * D, D])],
    "s7": [("wq", [D, D]), ("wqs", [D, D]), ("wk", [D, D]), ("wks", [D, D]),
           ("wv", [D, D]), ("wo", [D, D]), ("ropec", [D, T]), ("ropesq", [D, T])],
}


def _mk_stage_builder(name, body):
    def build():
        nc = bass.Bass("TRN2", target_bir_lowering=False)
        xin = nc.dram_tensor("xin", [D, T], F32, kind="ExternalInput").ap()
        w = {k: nc.dram_tensor(k, s, F32, kind="ExternalInput").ap()
             for k, s in STAGE_INPUTS[name]}
        xout = nc.dram_tensor("xout", [D, T], F32, kind="ExternalOutput").ap()
        es = ExitStack()
        with es:
            block = es.enter_context(nc.Block())
            ctx = Ctx(nc, block, es)
            body(ctx, nc, xin, xout, w, name + "_")
            ctx.barrier()
        return nc
    return build


_CACHE = {}


def _get_prog(name, builder):
    if name not in _CACHE:
        _CACHE[name] = builder()
    return _CACHE[name]


def stage1_weights(inp):
    return {k: np.ascontiguousarray(inp["L0_attn_a_" + k]) for k in
            ["ln1_g", "ln1_b", "wq", "wk", "wv", "wo", "ln2_g", "ln2_b",
             "cms_w1", "cms_b1", "cms_w2", "cms_b2"]}


STAGE_BODIES = {
    "s1": emit_stage1_body, "s2": emit_stage2_body, "s3": emit_stage3_body,
    "s5": emit_stage5_body, "s6": emit_stage6_body, "s7": emit_stage7_body,
}
STAGE_WEIGHTS = {
    "s1": stage1_weights, "s2": stage2_weights, "s3": stage3_weights,
    "s5": stage5_weights, "s6": stage6_weights, "s7": stage7_weights,
}
STAGES = [(n, _mk_stage_builder(n, STAGE_BODIES[n]), STAGE_WEIGHTS[n])
          for n in ["s1", "s2", "s3", "s5", "s6", "s7"]]
BY_NAME = {n: (b, w) for n, b, w in STAGES}
CHAIN = ["s1", "s2", "s3", "s2", "s5", "s6", "s7", "s6"]
FUSED = True


FUSE_CHAIN = None


def build_fused():
    nc = bass.Bass("TRN2", target_bir_lowering=False)
    xin = nc.dram_tensor("xin", [D, T], F32, kind="ExternalInput").ap()
    xout = nc.dram_tensor("xout", [D, T], F32, kind="ExternalOutput").ap()
    XA = nc.dram_tensor("XA", [D, T], F32, kind="Internal").ap()
    XB = nc.dram_tensor("XB", [D, T], F32, kind="Internal").ap()
    wAPs = {}
    for name in sorted(set(FUSE_CHAIN if FUSE_CHAIN is not None else CHAIN)):
        wAPs[name] = {k: nc.dram_tensor(f"{name}_{k}", s, F32, kind="ExternalInput").ap()
                      for k, s in STAGE_INPUTS[name]}
    es = ExitStack()
    with es:
        block = es.enter_context(nc.Block())
        ctx = Ctx(nc, block, es)
        chain = FUSE_CHAIN if FUSE_CHAIN is not None else CHAIN
        n = len(chain)
        for i, name in enumerate(chain):
            src_ = xin if i == 0 else (XA if i % 2 == 1 else XB)
            dst_ = xout if i == n - 1 else (XA if i % 2 == 0 else XB)
            STAGE_BODIES[name](ctx, nc, src_, dst_, wAPs[name], f"{name}_")
            ctx.barrier()
        ctx.barrier()
    return nc


def run_stage(name, X, inp):
    builder, wfn = BY_NAME[name]
    nc = _get_prog(name, builder)
    wmap = wfn(inp)
    in_maps = [dict(wmap, xin=X[b]) for b in range(8)]
    res = run_bass_kernel_spmd(nc, in_maps, core_ids=list(range(8)))
    return np.stack([r["xout"] for r in res.results])


def kernel(**inp):
    x = inp["L0_attn_a_x"]
    X = np.ascontiguousarray(np.asarray(x).transpose(0, 2, 1)).astype(np.float32)
    if FUSED and N_STAGES == 8:
        nc = _get_prog("fused", build_fused)
        wmap = {}
        for name in sorted(set(CHAIN)):
            for k, v in STAGE_WEIGHTS[name](inp).items():
                wmap[f"{name}_{k}"] = v
        in_maps = [dict(wmap, xin=X[b]) for b in range(8)]
        res = run_bass_kernel_spmd(nc, in_maps, core_ids=list(range(8)))
        X = np.stack([r["xout"] for r in res.results])
    else:
        for name in CHAIN[:N_STAGES]:
            X = run_stage(name, X, inp)
    return np.ascontiguousarray(X.transpose(0, 2, 1))
